# version 09
# baseline (speedup 1.0000x reference)
import numpy as np
from contextlib import ExitStack

import concourse.bass as bass
import concourse.mybir as mybir
from concourse.bass_utils import run_bass_kernel_spmd

F32 = mybir.dt.float32
BF16 = mybir.dt.bfloat16
F16 = mybir.dt.float16
I32 = mybir.dt.int32
AF = mybir.ActivationFunctionType
ALU = mybir.AluOpType
AX = mybir.AxisListType

T = 2048
D = 2048
NT = 16
KC = 16
I0 = 5632
NI = 44
O1 = 2 * I0
NO1 = 88
HD = 64
NH = 32

ENGINES = ("tensor", "vector", "scalar", "gpsimd", "sync")
DMAQ = {"sync": "dsync", "gpsimd": "dgp", "scalar": "dscal"}
ALL_SEMS = list(ENGINES) + list(DMAQ.values())

FP4C = 0xC8643210 - (1 << 32)


class KB:
    def __init__(self):
        self.stream = {e: [] for e in ENGINES}
        self.cnt = dict.fromkeys(ALL_SEMS, 0)
        self.writer = {}
        self.readers = {}
        self.guar = {e: {} for e in ENGINES}
        self.pending_t = {}
        self.n_ops = 0

    def _waits(self, engine, reads, writes):
        need = {}
        for k in reads:
            w = self.writer.get(k)
            if w is not None and need.get(w[0], -1) < w[1]:
                need[w[0]] = w[1]
        for k in writes:
            w = self.writer.get(k)
            if w is not None and need.get(w[0], -1) < w[1]:
                need[w[0]] = w[1]
            for s, v in self.readers.get(k, ()):
                if need.get(s, -1) < v:
                    need[s] = v
        g = self.guar[engine]
        for s in sorted(need):
            v = need[s]
            if s in DMAQ.values():
                v = self.cnt[s]
            if engine == "tensor" and s == "tensor":
                continue
            if g.get(s, 0) >= v:
                continue
            self.stream[engine].append(("w", s, v))
            g[s] = v

    def _register(self, sem, val, reads, writes):
        for k in writes:
            self.writer[k] = (sem, val)
            self.readers[k] = []
        for k in reads:
            self.readers.setdefault(k, []).append((sem, val))

    def op(self, engine, fn, reads=(), writes=(), inc=True):
        self._waits(engine, reads, writes)
        val = self.cnt[engine] + 1
        self.stream[engine].append(("o", fn, engine if inc else None))
        if inc:
            self.cnt[engine] = val
        self._register(engine, val, reads, writes)
        self.n_ops += 1

    def dma(self, engine, fn, reads=(), writes=()):
        self._waits(engine, reads, writes)
        q = DMAQ[engine]
        val = self.cnt[q] + 16
        self.cnt[q] = val
        self.stream[engine].append(("d", fn, q))
        self._register(q, val, reads, writes)
        self.n_ops += 1

    def dma_t(self, engine, fn, reads=(), writes=()):
        self.dma(engine, fn, reads=reads, writes=writes)

    def fence(self, engine, fn):
        pass

    def barrier(self):
        tgt = dict(self.cnt)
        for e in ENGINES:
            g = self.guar[e]
            for s, v in tgt.items():
                if v <= 0 or s == e or (e == "tensor" and s == "tensor"):
                    continue
                if g.get(s, 0) >= v:
                    continue
                self.stream[e].append(("w", s, v))
                g[s] = v

    def finalize(self, nc):
        with ExitStack() as es:
            sems = {s: es.enter_context(nc.semaphore("sem_" + s))
                    for s in ALL_SEMS}
            sems["tjunk"] = es.enter_context(nc.semaphore("sem_tjunk"))
            for s, v in self.cnt.items():
                if v > 0 and s != "sync":
                    self.stream["sync"].append(("w", s, v))
            block = es.enter_context(nc.Block())
            for e in ENGINES:
                def make(e):
                    def body(eng):
                        for it in self.stream[e]:
                            if it[0] == "w":
                                eng.wait_ge(sems[it[1]], it[2])
                            elif it[0] == "o":
                                inst = it[1](eng)
                                if it[2] is not None:
                                    inst.then_inc(sems[it[2]], 1)
                            elif it[0] == "t":
                                it[1](eng).then_inc(sems["tjunk"], 16)
                            else:
                                it[1](eng).then_inc(sems[it[2]], 16)
                    return body
                getattr(block, e)(make(e))


ALL_PHASES = ("P0", "P1", "P2", "P3", "DEC", "P4A", "P4B", "P4C", "P5", "P6")


def build_program(dbg=(), phases=ALL_PHASES, p1_tq=4, p1_parts='AB', p1_ib=22, p4b_pairs=16, p4b_parts='FTC', p5_tq=4, p5_parts='AB', p5b2=4):
    nc = bass.Bass("TRN2", target_bir_lowering=False)
    kb = KB()
    root = ExitStack()
    _stacks = []

    def mkstack():
        s = ExitStack()
        _stacks.append(s)
        return s

    def dram(name, shape, dt, inp=False):
        kind = "ExternalInput" if inp else (
            "ExternalOutput" if name in dbg else "Internal")
        return nc.dram_tensor(name, shape, dt, kind=kind).ap()

    x_in = dram("x", [T, D], F32, inp=True)
    wq0 = dram("wq0", [D, D], F32, inp=True)
    wk0 = dram("wk0", [D, 512], F32, inp=True)
    wv0 = dram("wv0", [D, 512], F32, inp=True)
    wo0 = dram("wo0", [D, D], F32, inp=True)
    gw0 = dram("gw0", [I0, D], F32, inp=True)
    uw0 = dram("uw0", [I0, D], F32, inp=True)
    dw0 = dram("dw0", [D, I0], F32, inp=True)
    gs0 = dram("gs0", [NI, 128], F32, inp=True)
    us0 = dram("us0", [NI, 128], F32, inp=True)
    ds0 = dram("ds0", [KC, 128], F32, inp=True)
    Wq1 = dram("Wq1", [D, D], F32, inp=True)
    Wk1 = dram("Wk1", [D, D], F32, inp=True)
    Wv1 = dram("Wv1", [D, D], F32, inp=True)
    Wo1 = dram("Wo1", [D, D], F32, inp=True)
    wfq1 = dram("wfq1", [NH * HD, HD], F32, inp=True)
    wfk1 = dram("wfk1", [NH * HD, HD], F32, inp=True)
    gup = dram("gup", [O1, 256], I32, inp=True)
    gus = dram("gus", [16, O1], F16, inp=True)
    dnp = dram("dnp", [D, 704], I32, inp=True)
    dns = dram("dns", [NI, D], F16, inp=True)
    out = nc.dram_tensor("out", [T, D], F32, kind="ExternalOutput").ap()

    c_gw = dram("c_gw", [I0, D], BF16)
    c_uw = dram("c_uw", [I0, D], BF16)
    c_dw = dram("c_dw", [D, I0], BF16)
    c_wq = dram("c_wq", [D, D], BF16)
    c_wk = dram("c_wk", [D, 512], BF16)
    c_wv = dram("c_wv", [D, 512], BF16)
    c_wo = dram("c_wo", [D, D], BF16)
    c_Wq = dram("c_Wq", [D, D], BF16)
    c_Wk = dram("c_Wk", [D, D], BF16)
    c_Wv = dram("c_Wv", [D, D], BF16)
    c_Wo = dram("c_Wo", [D, D], BF16)
    c_fq = dram("c_fq", [NH * HD, HD], BF16)
    c_fk = dram("c_fk", [NH * HD, HD], BF16)
    d_gu = dram("d_gu", [O1, D], BF16)
    d_dn = dram("d_dn", [D, I0], BF16)
    t_gw = dram("t_gw", [D, I0], BF16)
    t_uw = dram("t_uw", [D, I0], BF16)
    t_dw = dram("t_dw", [I0, D], BF16)
    t_dgu = dram("t_dgu", [D, O1], BF16)
    t_ddn = dram("t_ddn", [I0, D], BF16)
    t_Wq = dram("t_Wq", [D, D], BF16)
    t_Wk = dram("t_Wk", [D, D], BF16)
    t_Wv = dram("t_Wv", [D, D], BF16)
    t_Wo = dram("t_Wo", [D, D], BF16)
    a_xn0 = dram("a_xn0", [T, D], BF16)
    a_v0 = dram("a_v0", [512, T], BF16)
    a_v0T = dram("a_v0T", [T, 512], BF16)
    a_g0 = dram("a_g0", [D, T], BF16)
    a_sg0 = dram("a_sg0", [D, T], BF16)
    a_s0 = dram("a_s0", [D, T], BF16)
    a_ctx0 = dram("a_ctx0", [D, T], BF16)
    a_x1 = dram("a_x1", [T, D], F32)
    a_xn1 = dram("a_xn1", [T, D], BF16)
    a_q1 = dram("a_q1", [D, T], BF16)
    a_k1 = dram("a_k1", [D, T], BF16)
    a_v1 = dram("a_v1", [D, T], BF16)
    a_s1 = dram("a_s1", [D, T], BF16)
    a_g1 = dram("a_g1", [D, T], BF16)
    a_sg1 = dram("a_sg1", [D, T], BF16)

    def sb(stack, name, shape, dt, side="right"):
        return stack.enter_context(
            nc.sbuf_tensor(name, shape, dt, side=side))

    def ps(stack, name, shape, dt=F32):
        return stack.enter_context(nc.psum_tensor(name, shape, dt))

    iota_f = sb(root, "iota_f", [128, 128], F32, "left")
    identity = sb(root, "identity", [128, 128], BF16, "left")
    identF = sb(root, "identF", [128, 128], F32, "left")
    triu = sb(root, "triu", [128, 128], BF16, "left")
    ones_c = sb(root, "ones_c", [128, 1], BF16, "left")
    ones_r = sb(root, "ones_r", [1, 128], BF16, "left")
    ones_rf = sb(root, "ones_rf", [1, 128], F32, "left")
    gs_sb = sb(root, "gs_sb", [128, NI], F32, "left")
    us_sb = sb(root, "us_sb", [128, NI], F32, "left")
    ds_sb = sb(root, "ds_sb", [128, KC], F32, "left")
    junk = sb(root, "junk", [128, D], BF16, "left")
    stt = [{n: sb(root, f"st_{n}{b}", [128, 1], F32, "left") for n in
            ("s1", "s2", "m", "v2", "msq", "var", "std", "rstd", "mb")}
           for b in range(4)]

    kb.op("gpsimd", lambda g: g.iota(
        iota_f[:], [[1, 128]], base=0, channel_multiplier=-1,
        allow_small_or_imprecise_dtypes=True), writes=[("iota", 0)])
    kb.op("vector", lambda v: v.tensor_scalar(
        out=identity[:], in0=iota_f[:], scalar1=0.0, scalar2=None,
        op0=ALU.is_equal), reads=[("iota", 0)], writes=[("ident", 0)])
    kb.op("vector", lambda v: v.tensor_scalar(
        out=identF[:], in0=iota_f[:], scalar1=0.0, scalar2=None,
        op0=ALU.is_equal), reads=[("iota", 0)], writes=[("identF", 0)])
    kb.op("vector", lambda v: v.tensor_scalar(
        out=triu[:], in0=iota_f[:], scalar1=0.0, scalar2=None,
        op0=ALU.is_ge), reads=[("iota", 0)], writes=[("triu", 0)])
    kb.op("vector", lambda v: v.memset(ones_c[:], 1.0), writes=[("ones", 0)])
    kb.op("vector", lambda v: v.memset(ones_r[:], 1.0), writes=[("ones", 1)])
    kb.op("vector", lambda v: v.memset(ones_rf[:], 1.0), writes=[("ones", 2)])
    fence_dr = dram("fence_dr", [1, 64], F32)
    fence_sb = sb(root, "fence_sb", [1, 64], F32, "left")

    def fence_sync():
        pass

    tst = [sb(root, f"tst{b}", [128, 512], BF16, "left") for b in range(4)]
    emit_tpass_n = [0]

    def emit_tpass(src_d, src_key, dst_d, dst_key, R, C):
        for cb in range(C // 128):
            for rb in range(R // 512):
                b = emit_tpass_n[0] % 4
                emit_tpass_n[0] += 1
                kb.dma_t("sync", lambda s, cb=cb, rb=rb, b=b: s.dma_start_transpose(
                    tst[b][:], src_d[rb * 512:(rb + 1) * 512,
                                     cb * 128:(cb + 1) * 128]),
                    reads=[src_key], writes=[("tst", b)])
                kb.dma("gpsimd", lambda g, cb=cb, rb=rb, b=b: g.dma_start(
                    out=dst_d[cb * 128:(cb + 1) * 128,
                              rb * 512:(rb + 1) * 512], in_=tst[b][:]),
                    reads=[("tst", b)], writes=[dst_key])
    for i, v in enumerate((1e-5, 1e-6, 1e-12, 0.125, 1.0 / D)):
        ct = sb(root, f"cst{i}", [128, 1], F32, "left")
        kb.op("vector", lambda ve, ct=ct, v=v: ve.memset(ct[:], v),
              writes=[("cst", i)])
        nc.const_aps.register(F32, v, ct[:])
    kb.barrier()

    for dst, src, key in (
        (c_gw, gw0, "c_gw"), (c_uw, uw0, "c_uw"), (c_dw, dw0, "c_dw"),
        (c_wq, wq0, "c_wq"), (c_wk, wk0, "c_wk"), (c_wv, wv0, "c_wv"),
        (c_wo, wo0, "c_wo"), (c_Wq, Wq1, "c_Wq"), (c_Wk, Wk1, "c_Wk"),
        (c_Wv, Wv1, "c_Wv"), (c_Wo, Wo1, "c_Wo"),
        (c_fq, wfq1, "c_fq"), (c_fk, wfk1, "c_fk"),
    ):
        kb.dma("gpsimd", lambda g, dst=dst, src=src: g.dma_start(
            out=dst, in_=src), writes=[key])
    for sbuf, src, n, key in ((gs_sb, gs0, NI, "gs"), (us_sb, us0, NI, "us"),
                              (ds_sb, ds0, KC, "ds")):
        for c in range(n):
            kb.dma("gpsimd", lambda g, sbuf=sbuf, src=src, c=c: g.dma_start(
                out=sbuf[:, c:c + 1], in_=src[c]), writes=[(key, c)])

    emit_tpass(c_gw, "c_gw", t_gw, "t_gw", I0, D)
    emit_tpass(c_uw, "c_uw", t_uw, "t_uw", I0, D)
    emit_tpass(c_dw, "c_dw", t_dw, "t_dw", D, I0)
    emit_tpass(c_Wq, "c_Wq", t_Wq, "t_Wq", D, D)
    emit_tpass(c_Wk, "c_Wk", t_Wk, "t_Wk", D, D)
    emit_tpass(c_Wv, "c_Wv", t_Wv, "t_Wv", D, D)
    emit_tpass(c_Wo, "c_Wo", t_Wo, "t_Wo", D, D)

    class Roll:
        _n = 0

        def __init__(self, stack, shape, dt, n):
            Roll._n += 1
            self.t = [sb(stack, f"roll{Roll._n}_{i}", shape, dt)
                      for i in range(n)]
            self.n = n
            self.i = 0
            self.id = Roll._n

        def next(self):
            i = self.i % self.n
            self.i += 1
            return self.t[i], ("roll", self.id, i)

    def emit_ln(xap, xkey, dst_ap, dst_key, b):
        s = stt[b]
        kb.op("vector", lambda v, s=s, xap=xap: v.reduce_sum(
            s["s1"][:], xap, axis=AX.X), reads=[xkey], writes=[("ls1", b)])
        kb.op("scalar", lambda sc, s=s, xap=xap: sc.activation(
            out=junk[:], in_=xap, func=AF.Square, accum_out=s["s2"][:]),
            reads=[xkey], writes=[("ls2", b), ("junk", 0)])
        kb.op("vector", lambda v, s=s: v.tensor_scalar(
            out=s["m"][:], in0=s["s1"][:], scalar1=1.0 / D, scalar2=None,
            op0=ALU.mult), reads=[("ls1", b)], writes=[("lm", b)])
        kb.op("vector", lambda v, s=s: v.tensor_scalar(
            out=s["v2"][:], in0=s["s2"][:], scalar1=1.0 / D, scalar2=None,
            op0=ALU.mult), reads=[("ls2", b)], writes=[("lv2", b)])
        kb.op("vector", lambda v, s=s: v.tensor_tensor(
            out=s["msq"][:], in0=s["m"][:], in1=s["m"][:], op=ALU.mult),
            reads=[("lm", b)], writes=[("lmsq", b)])
        kb.op("vector", lambda v, s=s: v.tensor_tensor(
            out=s["var"][:], in0=s["v2"][:], in1=s["msq"][:],
            op=ALU.subtract), reads=[("lv2", b), ("lmsq", b)],
            writes=[("lvar", b)])
        kb.op("scalar", lambda sc, s=s: sc.activation(
            out=s["std"][:], in_=s["var"][:], func=AF.Sqrt, bias=1e-5),
            reads=[("lvar", b)], writes=[("lstd", b)])
        kb.op("vector", lambda v, s=s: v.reciprocal(
            s["rstd"][:], s["std"][:]), reads=[("lstd", b)],
            writes=[("lrstd", b)])
        kb.op("vector", lambda v, s=s: v.tensor_tensor(
            out=s["mb"][:], in0=s["m"][:], in1=s["rstd"][:], op=ALU.mult),
            reads=[("lm", b), ("lrstd", b)], writes=[("lmb0", b)])
        kb.op("vector", lambda v, s=s: v.tensor_scalar(
            out=s["mb"][:], in0=s["mb"][:], scalar1=-1.0, scalar2=None,
            op0=ALU.mult), reads=[("lmb0", b)], writes=[("lmb", b)])
        kb.op("vector", lambda v, s=s, xap=xap, dst_ap=dst_ap: v.tensor_scalar(
            out=dst_ap, in0=xap, scalar1=s["rstd"][:], scalar2=s["mb"][:],
            op0=ALU.mult, op1=ALU.add),
            reads=[xkey, ("lrstd", b), ("lmb", b)], writes=[dst_key])

    xn0T = None
    if "P0" in phases:
        st0 = mkstack()
        xbuf = [sb(st0, f"p0x{b}", [128, D], F32) for b in range(2)]
        xnst = [sb(st0, f"p0n{b}", [128, D], BF16) for b in range(2)]
        for t in range(NT):
            b = t % 2
            kb.dma("gpsimd", lambda g, t=t, b=b: g.dma_start(
                out=xbuf[b][:], in_=x_in[t * 128:(t + 1) * 128, :]),
                writes=[("p0x", b)])
            emit_ln(xbuf[b][:], ("p0x", b), xnst[b][:], ("p0n", b), b)
            kb.dma("gpsimd", lambda g, t=t, b=b: g.dma_start(
                out=a_xn0[t * 128:(t + 1) * 128, :], in_=xnst[b][:]),
                reads=[("p0n", b)], writes=["a_xn0"])
        kb.barrier()
        st0.close()
        build_program.sx0 = sx0 = mkstack()
        xn0T = [sb(sx0, f"xn0T_{k}", [128, T], BF16, "left")
                for k in range(KC)]
        for k in range(KC):
            kb.dma_t("sync", lambda s, k=k: s.dma_start_transpose(
                xn0T[k][:], a_xn0[:, k * 128:(k + 1) * 128]),
                reads=["a_xn0"], writes=[("xn0T", k)])
        fence_sync()

    if "P1" in phases:
        st1 = mkstack()
        hdn = [sb(st1, f"hdn{i}", [128, 512], BF16) for i in range(NI)]
        hg = [sb(st1, f"hg{b}", [128, 512], BF16) for b in range(2)]
        g0st = [sb(st1, f"g0st{b}", [128, 512], BF16) for b in range(2)]
        gwb = [sb(st1, f"gwb{s}", [128, KC * 256], BF16) for s in range(2)]
        uwb = [sb(st1, f"uwb{s}", [128, KC * 256], BF16) for s in range(2)]
        dwb = [sb(st1, f"dwb{s}", [128, 11 * 512], BF16) for s in range(2)]
        ps1 = mkstack()
        pG = ps(ps1, "pG", [128, 2048])
        pU = ps(ps1, "pU", [128, 2048])

        def load_gu(ib, slot):
            kb.dma("sync", lambda s, ib=ib, slot=slot: s.dma_start(
                out=gwb[slot][:].rearrange("p (k m) -> p k m", k=KC),
                in_=t_gw[:, ib * 256:(ib + 1) * 256].rearrange(
                    "(k p) m -> p k m", p=128)),
                reads=["t_gw"], writes=[("gwb", slot)])
            kb.dma("gpsimd", lambda g, ib=ib, slot=slot: g.dma_start(
                out=uwb[slot][:].rearrange("p (k m) -> p k m", k=KC),
                in_=t_uw[:, ib * 256:(ib + 1) * 256].rearrange(
                    "(k p) m -> p k m", p=128)),
                reads=["t_uw"], writes=[("uwb", slot)])

        def load_dw(gidx, slot):
            hb4, sg = gidx // 4, gidx % 4
            kb.dma("sync" if gidx % 2 == 0 else "gpsimd",
                   lambda s, hb4=hb4, sg=sg, slot=slot: s.dma_start(
                       out=dwb[slot][:].rearrange("p (j m) -> p j m", j=11),
                       in_=t_dw[sg * 11 * 128:(sg + 1) * 11 * 128,
                                hb4 * 512:(hb4 + 1) * 512].rearrange(
                           "(j p) m -> p j m", p=128)),
                   reads=["t_dw"], writes=[("dwb", slot)])

        for tq in range(p1_tq if "A" in p1_parts else 0):
            tsl = slice(tq * 512, (tq + 1) * 512)
            load_gu(0, 0)
            for ib in range(p1_ib):
                slot = ib % 2
                po = (ib % 2) * 1024
                for k in range(KC):
                    for i2 in range(2):
                        kb.op("tensor", lambda t, k=k, i2=i2, slot=slot,
                              tsl=tsl, po=po: t.matmul(
                                  pG[:, po + i2 * 512:po + (i2 + 1) * 512],
                                  gwb[slot][:, k * 256 + i2 * 128:
                                            k * 256 + (i2 + 1) * 128],
                                  xn0T[k][:, tsl],
                                  start=(k == 0), stop=(k == KC - 1)),
                              reads=[("gwb", slot), ("xn0T", k)],
                              writes=[("pG", (ib % 2) * 2 + i2)],
                              inc=(i2 == 1))
                        kb.op("tensor", lambda t, k=k, i2=i2, slot=slot,
                              tsl=tsl, po=po: t.matmul(
                                  pU[:, po + i2 * 512:po + (i2 + 1) * 512],
                                  uwb[slot][:, k * 256 + i2 * 128:
                                            k * 256 + (i2 + 1) * 128],
                                  xn0T[k][:, tsl],
                                  start=(k == 0), stop=(k == KC - 1)),
                              reads=[("uwb", slot), ("xn0T", k)],
                              writes=[("pU", (ib % 2) * 2 + i2)],
                              inc=(i2 == 1))
                if ib + 1 < p1_ib:
                    load_gu(ib + 1, (ib + 1) % 2)
                for i2 in range(2):
                    ic = ib * 2 + i2
                    hb = ic % 2
                    kb.op("scalar", lambda sc, ic=ic, i2=i2, hb=hb, po=po:
                          sc.activation(
                              out=hg[hb][:],
                              in_=pG[:, po + i2 * 512:po + (i2 + 1) * 512],
                              func=AF.Silu, scale=gs_sb[:, ic:ic + 1]),
                          reads=[("pG", (ib % 2) * 2 + i2), ("gs", ic)],
                          writes=[("hg", hb)])
                    kb.op("vector", lambda v, ic=ic, i2=i2, hb=hb, po=po:
                          v.scalar_tensor_tensor(
                              out=hdn[ic][:],
                              in0=pU[:, po + i2 * 512:po + (i2 + 1) * 512],
                              scalar=us_sb[:, ic:ic + 1], in1=hg[hb][:],
                              op0=ALU.mult, op1=ALU.mult),
                          reads=[("pU", (ib % 2) * 2 + i2), ("us", ic),
                                 ("hg", hb)],
                          writes=[("hdn", ic)])
            if "B" in p1_parts:
                load_dw(0, 0)
                for gidx in range(16):
                    hb4, sg = gidx // 4, gidx % 4
                    slot = gidx % 2
                    for j in range(11):
                        i = sg * 11 + j
                        for h4 in range(4):
                            kb.op("tensor", lambda t, j=j, h4=h4, i=i,
                                  slot=slot: t.matmul(
                                      pG[:, h4 * 512:(h4 + 1) * 512],
                                      dwb[slot][:, j * 512 + h4 * 128:
                                                j * 512 + (h4 + 1) * 128],
                                      hdn[i][:], start=(i == 0),
                                      stop=(i == NI - 1)),
                                  reads=[("dwb", slot), ("hdn", i)],
                                  writes=[("pG", h4)], inc=(h4 == 3))
                    if gidx + 1 < 16:
                        load_dw(gidx + 1, (gidx + 1) % 2)
                    if sg == 3:
                        for h4 in range(4):
                            ht = hb4 * 4 + h4
                            b = ht % 2
                            kb.op("scalar", lambda sc, h4=h4, ht=ht, b=b:
                                  sc.activation(
                                      out=g0st[b][:],
                                      in_=pG[:, h4 * 512:(h4 + 1) * 512],
                                      func=AF.Silu,
                                      scale=ds_sb[:, ht:ht + 1]),
                                  reads=[("pG", h4), ("ds", ht)],
                                  writes=[("g0st", b)])
                            kb.dma("gpsimd", lambda g, ht=ht, tq=tq, b=b:
                                   g.dma_start(
                                       out=a_g0[ht * 128:(ht + 1) * 128,
                                                tq * 512:(tq + 1) * 512],
                                       in_=g0st[b][:]),
                                   reads=[("g0st", b)], writes=["a_g0"])
        kb.barrier()
        ps1.close()
        st1.close()

    def emit_mm4(load, n_out, rhs, rhs_key, evac, pA_, pB_, pAk, pBk,
                 tqs=4, transposing=False, bigsrc=None, bigkey=None,
                 bigbuf=None):
        roll_local = emit_mm4.roll
        first = True
        for tq in range(tqs):
            for mb in range(n_out // 512):
                psl, pkey = (pA_, pAk) if mb % 2 == 0 else (pB_, pBk)
                slot = mb % 2
                if bigsrc is not None:
                    if first:
                        kb.dma("sync", lambda s, mb=mb, slot=slot: s.dma_start(
                            out=bigbuf[slot][:].rearrange(
                                "p (k m) -> p k m", k=KC),
                            in_=bigsrc[:, mb * 512:(mb + 1) * 512].rearrange(
                                "(k p) m -> p k m", p=128)),
                            reads=[bigkey], writes=[("wbig", id(bigbuf), slot)])
                        first = False
                    for k in range(KC):
                        tkey = ("wbig", id(bigbuf), slot)
                        for m4 in range(4):
                            kb.op("tensor", lambda t, m4=m4, k=k, tq=tq,
                                  psl=psl, slot=slot: t.matmul(
                                      psl[:, m4 * 512:(m4 + 1) * 512],
                                      bigbuf[slot][:, k * 512 + m4 * 128:
                                                   k * 512 + (m4 + 1) * 128],
                                      rhs[k][:, tq * 512:(tq + 1) * 512],
                                      start=(k == 0), stop=(k == KC - 1)),
                                  reads=[tkey, (rhs_key, k)],
                                  writes=[(pkey, m4)], inc=(m4 == 3))
                    nmb = mb + 1 if mb + 1 < n_out // 512 else (
                        0 if tq + 1 < tqs else None)
                    if nmb is not None:
                        ns = nmb % 2
                        kb.dma("sync", lambda s, nmb=nmb, ns=ns: s.dma_start(
                            out=bigbuf[ns][:].rearrange(
                                "p (k m) -> p k m", k=KC),
                            in_=bigsrc[:, nmb * 512:(nmb + 1) * 512].rearrange(
                                "(k p) m -> p k m", p=128)),
                            reads=[bigkey], writes=[("wbig", id(bigbuf), ns)])
                else:
                    for k in range(KC):
                        tl, tkey = roll_local.next()
                        load(tl, tkey, k, mb)
                        for m4 in range(4):
                            kb.op("tensor", lambda t, tl=tl, m4=m4, k=k,
                                  tq=tq, psl=psl: t.matmul(
                                      psl[:, m4 * 512:(m4 + 1) * 512],
                                      tl[:, m4 * 128:(m4 + 1) * 128],
                                      rhs[k][:, tq * 512:(tq + 1) * 512],
                                      start=(k == 0), stop=(k == KC - 1)),
                                  reads=[tkey, (rhs_key, k)],
                                  writes=[(pkey, m4)], inc=(m4 == 3))
                for m4 in range(4):
                    evac(mb * 4 + m4, tq, psl[:, m4 * 512:(m4 + 1) * 512],
                         (pkey, m4))

    if "P2" in phases:
        sq_stack = mkstack()
        skT = mkstack()
        st2 = mkstack()
        q_sb = [sb(sq_stack, f"q_sb{m}", [128, T], BF16) for m in range(16)]
        kT = [sb(skT, f"kT{m}", [128, T], BF16) for m in range(4)]
        vst = [sb(st2, f"vst{b}", [128, 512], BF16) for b in range(2)]
        wbig2 = [sb(st2, f"wbig2_{s}", [128, KC * 512], BF16)
                 for s in range(2)]
        emit_mm4.roll = None
        ps2 = mkstack()
        pA_ = ps(ps2, "p2A", [128, 2048])
        pB_ = ps(ps2, "p2B", [128, 2048])

        def mkload(src, skey):
            def load(tl, tkey, k, mb):
                kb.dma("gpsimd", lambda g, tl=tl, k=k, mb=mb: g.dma_start(
                    out=tl[:], in_=src[k * 128:(k + 1) * 128,
                                       mb * 512:(mb + 1) * 512]),
                    reads=[skey], writes=[tkey])
            return load

        def ev_q(m, tq, psl, pk):
            kb.op("vector", lambda v, m=m, tq=tq, psl=psl: v.tensor_copy(
                out=q_sb[m][:, tq * 512:(tq + 1) * 512], in_=psl),
                reads=[pk], writes=[("q", m)])

        def ev_k(m, tq, psl, pk):
            kb.op("vector", lambda v, m=m, tq=tq, psl=psl: v.tensor_copy(
                out=kT[m][:, tq * 512:(tq + 1) * 512], in_=psl),
                reads=[pk], writes=[("kT", m)])

        def ev_v(m, tq, psl, pk):
            b = m % 2
            kb.op("vector", lambda v, psl=psl, b=b: v.tensor_copy(
                out=vst[b][:], in_=psl), reads=[pk], writes=[("vst", b)])
            kb.dma("gpsimd", lambda g, m=m, tq=tq, b=b: g.dma_start(
                out=a_v0[m * 128:(m + 1) * 128, tq * 512:(tq + 1) * 512],
                in_=vst[b][:]), reads=[("vst", b)], writes=["a_v0"])

        emit_mm4(None, D, xn0T, "xn0T", ev_q, pA_, pB_,
                 "p2A", "p2B", bigsrc=c_wq, bigkey="c_wq", bigbuf=wbig2)
        emit_mm4(None, 512, xn0T, "xn0T", ev_k, pA_, pB_,
                 "p2A", "p2B", bigsrc=c_wk, bigkey="c_wk", bigbuf=wbig2)
        emit_mm4(None, 512, xn0T, "xn0T", ev_v, pA_, pB_,
                 "p2A", "p2B", bigsrc=c_wv, bigkey="c_wv", bigbuf=wbig2)
        for t in range(NT):
            kb.dma_t("sync", lambda s, t=t: s.dma_start_transpose(
                vst[0][:], a_v0[:, t * 128:(t + 1) * 128]),
                reads=["a_v0"], writes=[("vst", 0)])
            fence_sync()
            kb.dma("gpsimd", lambda g, t=t: g.dma_start(
                out=a_v0T[t * 128:(t + 1) * 128, :], in_=vst[0][:]),
                reads=[("vst", 0)], writes=["a_v0T"])
        kb.barrier()
        ps2.close()
        st2.close()
        build_program.sx0.close()

    def make_decode(std, pSC, pSCkey):
        gusl = sb(std, "gusl", [16, 1408], F16)
        dnsb = sb(std, "dnsb", [NI, D], F16)
        identH = sb(std, "identH", [128, 128], F16)
        scT = sb(std, "scT", [128, NO1 * 16], F32)
        scTd = sb(std, "scTd", [128, KC * NI], F32)
        kb.op("vector", lambda v: v.tensor_scalar(
            out=identH[:], in0=iota_f[:], scalar1=0.0, scalar2=None,
            op0=ALU.is_equal), reads=[("iota", 0)], writes=[("identH", 0)])
        kb.dma("sync", lambda s: s.dma_start(out=dnsb[:], in_=dns),
               writes=[("dnsb", 0)])
        for sl in range(8):
            kb.dma("sync", lambda s, sl=sl: s.dma_start(
                out=gusl[:], in_=gus[:, sl * 1408:(sl + 1) * 1408]),
                reads=[("gusl_done", 0)], writes=[("gusl", 0)])
            for cc in range(11):
                c = sl * 11 + cc
                kb.op("tensor", lambda t, cc=cc: t.matmul(
                    pSC[:, 0:16], gusl[:, cc * 128:(cc + 1) * 128],
                    identH[0:16, 0:16], start=True, stop=True),
                    reads=[("gusl", 0), ("identH", 0)], writes=[pSCkey])
                kb.op("vector", lambda v, c=c: v.tensor_copy(
                    out=scT[:, c * 16:(c + 1) * 16], in_=pSC[:, 0:16]),
                    reads=[pSCkey], writes=[("scT", 0), ("gusl_done", 0)])
        for c in range(KC):
            kb.op("tensor", lambda t, c=c: t.matmul(
                pSC[:, 0:NI], dnsb[:, c * 128:(c + 1) * 128],
                identH[0:NI, 0:NI], start=True, stop=True),
                reads=[("dnsb", 0), ("identH", 0)], writes=[pSCkey])
            kb.op("vector", lambda v, c=c: v.tensor_copy(
                out=scTd[:, c * NI:(c + 1) * NI], in_=pSC[:, 0:NI]),
                reads=[pSCkey], writes=[("scTd", 0)])

        maxw = 352
        pkt = [sb(std, f"pkt{b}", [128, maxw], I32) for b in range(2)]
        cfull = sb(std, "cfull", [128, maxw], I32)
        tmp = {}
        for n in ("sh", "v2a", "v2", "s8", "sg"):
            tmp[n] = sb(std, f"dc_{n}", [128, maxw],
                        F32 if n == "sg" else I32)
        kb.op("vector", lambda v: v.memset(cfull[:], FP4C),
              writes=[("cf", 0)])

        def tiles(packed, pkey, nwf, nsub, nch, dscr, dkey, scs, sck,
                  scnf, scp):
            nw = nwf // nsub
            scn = scnf // nsub
            dect = [sb(std, f"dect{dkey}{b}", [128, 8 * nw], BF16)
                    for b in range(2)]
            for ch in range(nch * nsub):
                c, h = ch // nsub, ch % nsub
                b = ch % 2
                kb.dma("sync", lambda s, c=c, h=h, b=b, nw=nw,
                       packed=packed: s.dma_start(
                           out=pkt[b][:, 0:nw],
                           in_=packed[c * 128:(c + 1) * 128,
                                      h * nw:(h + 1) * nw]),
                       reads=[pkey], writes=[("pkt", b)])
                for j in range(8):
                    sh = tmp["sh"][:, 0:nw]
                    v2a = tmp["v2a"][:, 0:nw]
                    v2 = tmp["v2"][:, 0:nw]
                    s8 = tmp["s8"][:, 0:nw]
                    sg = tmp["sg"][:, 0:nw]
                    ekv = ("dc", "v")
                    ekp = ("dc", "p")
                    if j == 0:
                        kb.op("vector", lambda x, b=b, sh=sh, nw=nw:
                              x.tensor_scalar(
                                  out=sh, in0=pkt[b][:, 0:nw], scalar1=2,
                                  scalar2=28, op0=ALU.logical_shift_left,
                                  op1=ALU.bitwise_and),
                              reads=[("pkt", b)], writes=[ekv])
                    else:
                        kb.op("vector", lambda x, b=b, j=j, sh=sh, nw=nw:
                              x.tensor_scalar(
                                  out=sh, in0=pkt[b][:, 0:nw],
                                  scalar1=4 * j - 2, scalar2=28,
                                  op0=ALU.logical_shift_right,
                                  op1=ALU.bitwise_and),
                              reads=[("pkt", b)], writes=[ekv])
                    kb.op("vector", lambda x, sh=sh, v2a=v2a, nw=nw:
                          x.tensor_tensor(
                              out=v2a, in0=cfull[:, 0:nw], in1=sh,
                              op=ALU.logical_shift_right),
                          reads=[ekv, ("cf", 0)], writes=[ekv])
                    kb.op("vector", lambda x, v2a=v2a, v2=v2:
                          x.tensor_scalar(
                              out=v2, in0=v2a, scalar1=15, scalar2=None,
                              op0=ALU.bitwise_and), reads=[ekv],
                          writes=[ekv])
                    kb.op("vector", lambda x, b=b, j=j, s8=s8, nw=nw:
                          x.tensor_scalar(
                              out=s8, in0=pkt[b][:, 0:nw], scalar1=4 * j,
                              scalar2=8, op0=ALU.logical_shift_right,
                              op1=ALU.bitwise_and),
                          reads=[("pkt", b)], writes=[ekv])
                    kb.op("vector", lambda x, s8=s8, sg=sg: x.tensor_scalar(
                        out=sg, in0=s8, scalar1=-0.125, scalar2=0.5,
                        op0=ALU.mult, op1=ALU.add), reads=[ekv],
                        writes=[ekp])
                    kb.op("vector", lambda x, v2=v2, sg=sg, b=b, j=j,
                          dect=dect: x.tensor_tensor(
                              out=dect[b][:, j::8], in0=v2, in1=sg,
                              op=ALU.mult),
                          reads=[ekv, ekp], writes=[("dect", dkey, b)])
                for grp in range(scn):
                    gi = c * scp + h * scn + grp
                    kb.op("scalar", lambda sc, b=b, gi=gi, grp=grp, scs=scs,
                          dect=dect: sc.mul(
                              out=dect[b][:, grp * 128:(grp + 1) * 128],
                              in_=dect[b][:, grp * 128:(grp + 1) * 128],
                              mul=scs[:, gi:gi + 1]),
                          reads=[sck, ("dect", dkey, b)],
                          writes=[("dect", dkey, b)])
                kb.dma("gpsimd", lambda g, c=c, h=h, b=b, nw=nw, dscr=dscr,
                       dect=dect: g.dma_start(
                           out=dscr[c * 128:(c + 1) * 128,
                                    h * 8 * nw:(h + 1) * 8 * nw],
                           in_=dect[b][:, 0:8 * nw]),
                       reads=[("dect", dkey, b)], writes=[dkey])
                yield

        def gen():
            yield from tiles(gup, "gup", 256, 1, NO1, d_gu, "d_gu", scT,
                             ("scT", 0), 16, 16)
            yield from tiles(dnp, "dnp", 704, 2, KC, d_dn, "d_dn", scTd,
                             ("scTd", 0), NI, NI)
        return gen()

    if "P3" in phases:
        st3 = mkstack()
        sq2 = sb(st3, "sq2", [64, T], BF16)
        kh = sb(st3, "kh", [64, T], BF16)
        qh = sb(st3, "qh", [64, T], BF16)
        normq = sb(st3, "normq", [1, T], F32)
        normk = sb(st3, "normk", [1, T], F32)
        kmax = sb(st3, "kmax", [1, 1], F32)
        negk8 = sb(st3, "negk8", [1, 1], F32)
        enB = sb(st3, "enB", [1, T], BF16)
        eBb = sb(st3, "eBb", [128, T], BF16)
        von = sb(st3, "von", [128, NT * 128], BF16)
        est = sb(st3, "est", [128, T], BF16)
        att = sb(st3, "att", [128, T], BF16)
        rec = sb(st3, "rec", [64, T], F32)
        ps3 = mkstack()
        pS = ps(ps3, "pS", [128, 2048])
        pC = ps(ps3, "pC", [128, 2048])
        std = mkstack()
        dec_gen = make_decode(std, pS, ("pS", 0)) if "DEC" in phases else None
        kb.op("vector", lambda v: v.memset(von[:], 1.0), writes=[("von", 0)])
        for g in range(8):
            if dec_gen is not None:
                for _ in range(13):
                    if next(dec_gen, "END") == "END":
                        dec_gen = None
                        break
            for c in range(NT):
                kb.dma("gpsimd", lambda gg, g=g, c=c: gg.dma_start(
                    out=von[:, c * 128:c * 128 + 64],
                    in_=a_v0T[c * 128:(c + 1) * 128,
                              g * 64:(g + 1) * 64]),
                    reads=["a_v0T"], writes=[("von", 0)])
            kb.op("vector", lambda v, g=g: v.tensor_copy(
                out=kh[:], in_=kT[g // 2][64 * (g % 2):64 * (g % 2) + 64, :]),
                reads=[("kT", g // 2)], writes=[("kh", 0)])
            kb.op("scalar", lambda sc: sc.activation(
                out=sq2[:], in_=kh[:],
                func=AF.Square), reads=[("kh", 0)], writes=[("sq2", 0)])
            for n in range(4):
                kb.op("tensor", lambda t, n=n: t.matmul(
                    pS[0:1, n * 512:(n + 1) * 512], ones_c[0:64, :],
                    sq2[:, n * 512:(n + 1) * 512], start=True, stop=True),
                    reads=[("sq2", 0), ("ones", 0)], writes=[("pS", n)],
                    inc=(n == 3))
            kb.op("scalar", lambda sc: sc.activation(
                out=normk[:], in_=pS[0:1, :], func=AF.Sqrt),
                reads=[("pS", 0), ("pS", 1), ("pS", 2), ("pS", 3)],
                writes=[("normk", 0)])
            kb.op("vector", lambda v: v.reduce_max(
                kmax[:], normk[:], axis=AX.X), reads=[("normk", 0)],
                writes=[("kmax", 0)])
            kb.op("vector", lambda v: v.tensor_scalar(
                out=negk8[:], in0=kmax[:], scalar1=-0.125, scalar2=None,
                op0=ALU.mult), reads=[("kmax", 0)], writes=[("negk8", 0)])
            for r in range(4):
                H = g * 4 + r
                hq, ho = H // 2, 64 * (H % 2)
                kb.op("vector", lambda v, hq=hq, ho=ho: v.tensor_copy(
                    out=qh[:], in_=q_sb[hq][ho:ho + 64, :]),
                    reads=[("q", hq)], writes=[("qh", 0)])
                kb.op("scalar", lambda sc: sc.activation(
                    out=sq2[:], in_=qh[:], func=AF.Square),
                    reads=[("qh", 0)], writes=[("sq2", 0)])
                for n in range(4):
                    kb.op("tensor", lambda t, n=n: t.matmul(
                        pS[0:1, n * 512:(n + 1) * 512], ones_c[0:64, :],
                        sq2[:, n * 512:(n + 1) * 512], start=True,
                        stop=True),
                        reads=[("sq2", 0), ("ones", 0)], writes=[("pS", n)],
                        inc=(n == 3))
                kb.op("scalar", lambda sc: sc.activation(
                    out=normq[:], in_=pS[0:1, :], func=AF.Sqrt),
                    reads=[("pS", 0), ("pS", 1), ("pS", 2), ("pS", 3)],
                    writes=[("normq", 0)])
                kb.op("scalar", lambda sc: sc.activation(
                    out=enB[:], in_=normq[:], func=AF.Exp, scale=negk8[:]),
                    reads=[("normq", 0), ("negk8", 0)], writes=[("enB", 0)])
                for n in range(4):
                    kb.op("tensor", lambda t, n=n: t.matmul(
                        pC[:, n * 512:(n + 1) * 512], ones_r[:],
                        enB[:, n * 512:(n + 1) * 512], start=True,
                        stop=True),
                        reads=[("enB", 0), ("ones", 1)], writes=[("pC", n)],
                        inc=(n == 3))
                kb.op("vector", lambda v: v.tensor_copy(
                    out=eBb[:], in_=pC[:]),
                    reads=[("pC", 0), ("pC", 1), ("pC", 2), ("pC", 3)],
                    writes=[("eBb", 0)])
                for kc in range(KC):
                    col0 = kc * 128
                    col = col0
                    while col < T:
                        w = min(512 - col % 512, T - col)
                        kb.op("tensor", lambda t, kc=kc, col=col, w=w:
                              t.matmul(
                                  pS[:, col:col + w],
                                  kh[:, kc * 128:(kc + 1) * 128],
                                  qh[:, col:col + w],
                                  start=True, stop=True),
                              reads=[("kh", 0), ("qh", 0)],
                              writes=[("pS", col // 512)],
                              inc=(col + w >= T))
                        col += w
                    rdk = [("pS", n) for n in range(col0 // 512, 4)]
                    kb.op("scalar", lambda sc, col0=col0: sc.activation(
                        out=est[:, col0:T], in_=pS[:, col0:T], func=AF.Exp,
                        scale=0.125), reads=rdk, writes=[("est", 0)])
                    kb.op("vector", lambda v, col0=col0: v.tensor_tensor(
                        out=att[:, col0:T], in0=est[:, col0:T],
                        in1=eBb[:, col0:T], op=ALU.mult),
                        reads=[("est", 0), ("eBb", 0)], writes=[("att", 0)])
                    kb.op("vector", lambda v, col0=col0: v.tensor_tensor(
                        out=att[:, col0:col0 + 128],
                        in0=att[:, col0:col0 + 128], in1=triu[:],
                        op=ALU.mult),
                        reads=[("att", 0), ("triu", 0)], writes=[("att", 0)])
                    col = col0
                    while col < T:
                        w = min(512 - col % 512, T - col)
                        nb = col // 512
                        kb.op("tensor", lambda t, kc=kc, col=col, w=w,
                              nb=nb: t.matmul(
                                  pC[:, col:col + w],
                                  von[:, kc * 128:(kc + 1) * 128],
                                  att[:, col:col + w], start=(kc == 0),
                                  stop=(kc == nb * 4 + 3)),
                              reads=[("von", 0), ("att", 0)],
                              writes=[("pC", nb)],
                              inc=(kc == nb * 4 + 3))
                        col += w
                kb.op("vector", lambda v: v.reciprocal(
                    rec[:], pC[64:128, :]),
                    reads=[("pC", 0), ("pC", 1), ("pC", 2), ("pC", 3)],
                    writes=[("rec", 0)])
                kb.op("vector", lambda v, hq=hq, ho=ho: v.tensor_tensor(
                    out=q_sb[hq][ho:ho + 64, :], in0=pC[0:64, :],
                    in1=rec[:], op=ALU.mult),
                    reads=[("pC", 0), ("pC", 1), ("pC", 2), ("pC", 3),
                           ("rec", 0)],
                    writes=[("q", hq)])
        if "a_ctx0" in dbg:
            for k in range(KC):
                kb.dma("gpsimd", lambda g, k=k: g.dma_start(
                    out=a_ctx0[k * 128:(k + 1) * 128, :], in_=q_sb[k][:]),
                    reads=[("q", k)], writes=["a_ctx0"])
        if dec_gen is not None:
            for _ in dec_gen:
                pass
        kb.barrier()
        ps3.close()
        std.close()
        st3.close()
        skT.close()
        st3b = mkstack()
        g0t = [sb(st3b, f"g0t{b}", [128, T], BF16) for b in range(2)]
        sgst = [sb(st3b, f"sgst{b}", [128, T], BF16) for b in range(2)]
        wbig3 = [sb(st3b, f"wbig3_{s}", [128, KC * 512], BF16)
                 for s in range(2)]
        ps3b = mkstack()
        pA3 = ps(ps3b, "p3A", [128, 2048])
        pB3 = ps(ps3b, "p3B", [128, 2048])
        kb.dma("sync", lambda s: s.dma_start(
            out=wbig3[0][:].rearrange("p (k m) -> p k m", k=KC),
            in_=c_wo[:, 0:512].rearrange("(k p) m -> p k m", p=128)),
            reads=["c_wo"], writes=[("wbig3", 0)])
        for mb in range(4):
            slot = mb % 2
            for m4 in range(4):
                m = mb * 4 + m4
                b = m % 2
                psm, pkm = (pA3, "p3A") if m % 2 == 0 else (pB3, "p3B")
                kb.dma("gpsimd", lambda g, m=m, b=b: g.dma_start(
                    out=g0t[b][:], in_=a_g0[m * 128:(m + 1) * 128, :]),
                    reads=["a_g0"], writes=[("g0t", b)])
                for k in range(KC):
                    for n in range(4):
                        kb.op("tensor", lambda t, m4=m4, n=n, k=k,
                              psm=psm, slot=slot: t.matmul(
                                  psm[:, n * 512:(n + 1) * 512],
                                  wbig3[slot][:, k * 512 + m4 * 128:
                                              k * 512 + (m4 + 1) * 128],
                                  q_sb[k][:, n * 512:(n + 1) * 512],
                                  start=(k == 0), stop=(k == KC - 1)),
                              reads=[("wbig3", slot), ("q", k)],
                              writes=[(pkm, n)],
                              inc=(n == 3))
                if m4 == 0 and mb + 1 < 4:
                    ns = (mb + 1) % 2
                    kb.dma("sync", lambda s, mb=mb, ns=ns: s.dma_start(
                        out=wbig3[ns][:].rearrange("p (k m) -> p k m", k=KC),
                        in_=c_wo[:, (mb + 1) * 512:(mb + 2) * 512].rearrange(
                            "(k p) m -> p k m", p=128)),
                        reads=["c_wo"], writes=[("wbig3", ns)])
                if "a_s0" in dbg:
                    kb.op("scalar", lambda sc, b=b, psm=psm: sc.activation(
                        out=g1dbg[b][:], in_=psm[:], func=AF.Copy),
                        reads=[(pkm, 0), (pkm, 1), (pkm, 2), (pkm, 3)],
                        writes=[("g1dbg", b)])
                    kb.dma("gpsimd", lambda g, m=m, b=b: g.dma_start(
                        out=a_s0[m * 128:(m + 1) * 128, :], in_=g1dbg[b][:]),
                        reads=[("g1dbg", b)], writes=["a_s0"])
                kb.op("vector", lambda v, b=b, psm=psm: v.tensor_tensor(
                    out=sgst[b][:], in0=psm[:], in1=g0t[b][:], op=ALU.mult),
                    reads=[(pkm, 0), (pkm, 1), (pkm, 2), (pkm, 3),
                           ("g0t", b)],
                    writes=[("sgst", b)])
                kb.dma("gpsimd", lambda g, m=m, b=b: g.dma_start(
                    out=a_sg0[m * 128:(m + 1) * 128, :], in_=sgst[b][:]),
                    reads=[("sgst", b)], writes=["a_sg0"])
        kb.barrier()
        ps3b.close()
        st3b.close()
        st3c = mkstack()
        xb2 = [sb(st3c, f"xb2{b}", [128, D], F32) for b in range(2)]
        sgT = [sb(st3c, f"sgT{b}", [128, D], BF16) for b in range(2)]
        x1t = [sb(st3c, f"x1t{b}", [128, D], F32) for b in range(2)]
        xn2 = [sb(st3c, f"xn2{b}", [128, D], BF16) for b in range(2)]
        for t in range(NT):
            b = t % 2
            kb.dma("gpsimd", lambda g, t=t, b=b: g.dma_start(
                out=xb2[b][:], in_=x_in[t * 128:(t + 1) * 128, :]),
                writes=[("xb2", b)])
            kb.dma_t("sync", lambda s, t=t, b=b: s.dma_start_transpose(
                sgT[b][:], a_sg0[:, t * 128:(t + 1) * 128]),
                reads=["a_sg0"], writes=[("sgT", b)])
            fence_sync()
            kb.op("vector", lambda v, b=b: v.tensor_tensor(
                out=x1t[b][:], in0=xb2[b][:], in1=sgT[b][:], op=ALU.add),
                reads=[("xb2", b), ("sgT", b)], writes=[("x1t", b)])
            kb.dma("gpsimd", lambda g, t=t, b=b: g.dma_start(
                out=a_x1[t * 128:(t + 1) * 128, :], in_=x1t[b][:]),
                reads=[("x1t", b)], writes=["a_x1"])
            emit_ln(x1t[b][:], ("x1t", b), xn2[b][:], ("xn2", b), 2 + b)
            kb.dma("gpsimd", lambda g, t=t, b=b: g.dma_start(
                out=a_xn1[t * 128:(t + 1) * 128, :], in_=xn2[b][:]),
                reads=[("xn2", b)], writes=["a_xn1"])
        kb.barrier()
        st3c.close()

    xn1T = None
    if "P4A" in phases:
        sx1 = mkstack()
        xn1T = [sb(sx1, f"xn1T_{k}", [128, T], BF16, "left")
                for k in range(KC)]
        for k in range(KC):
            kb.dma_t("sync", lambda s, k=k: s.dma_start_transpose(
                xn1T[k][:], a_xn1[:, k * 128:(k + 1) * 128]),
                reads=["a_xn1"], writes=[("xn1T", k)])
        fence_sync()
        st4a = mkstack()
        pst = [sb(st4a, f"pst{b}", [128, 512], BF16) for b in range(2)]
        wbigA = [sb(st4a, f"wbigA_{s}", [128, KC * 512], BF16)
                 for s in range(2)]
        emit_mm4.roll = None
        ps4a = mkstack()
        pA4 = ps(ps4a, "p4A", [128, 2048])
        pB4 = ps(ps4a, "p4B", [128, 2048])

        def mkevs(dst, dk):
            def ev(m, tq, psl, pk):
                b = m % 2
                kb.op("vector", lambda v, psl=psl, b=b: v.tensor_copy(
                    out=pst[b][:], in_=psl), reads=[pk], writes=[("pst", b)])
                kb.dma("gpsimd", lambda g, m=m, tq=tq, b=b: g.dma_start(
                    out=dst[m * 128:(m + 1) * 128,
                            tq * 512:(tq + 1) * 512],
                    in_=pst[b][:]), reads=[("pst", b)], writes=[dk])
            return ev

        emit_mm4(None, D, xn1T, "xn1T",
                 mkevs(a_q1, "a_q1"), pA4, pB4, "p4A", "p4B",
                 bigsrc=t_Wq, bigkey="t_Wq", bigbuf=wbigA)
        emit_mm4(None, D, xn1T, "xn1T",
                 mkevs(a_k1, "a_k1"), pA4, pB4, "p4A", "p4B",
                 bigsrc=t_Wk, bigkey="t_Wk", bigbuf=wbigA)
        emit_mm4(None, D, xn1T, "xn1T",
                 mkevs(a_v1, "a_v1"), pA4, pB4, "p4A", "p4B",
                 bigsrc=t_Wv, bigkey="t_Wv", bigbuf=wbigA)
        emit_tpass(d_gu, "d_gu", t_dgu, "t_dgu", O1, D)
        emit_tpass(d_dn, "d_dn", t_ddn, "t_ddn", D, I0)
        kb.barrier()
        ps4a.close()
        st4a.close()

    if "P4B" in phases:
        st4 = mkstack()
        ctx1 = q_sb
        qp = sb(st4, "qp", [128, T], BF16)
        kp = sb(st4, "kp", [128, T], BF16)
        vp = sb(st4, "vp", [128, T], BF16)
        wfb = sb(st4, "wfb", [128, 2 * HD], BF16)
        fzh = sb(st4, "fzh", [128, NT * 128], BF16)
        qh4 = sb(st4, "qh4", [64, T], BF16)
        kh4 = sb(st4, "kh4", [64, T], BF16)
        wfh = sb(st4, "wfh", [64, 2 * HD], BF16)
        ez = sb(st4, "ez", [128, 128], F32)
        zmax = sb(st4, "zmax", [128, 1], F32)
        nzmax = sb(st4, "nzmax", [128, 1], F32)
        s4 = sb(st4, "s4", [128, 2], F32)
        r4 = sb(st4, "r4", [128, 2], F32)
        fqT = sb(st4, "fqT", [64, T], BF16)
        fkT = sb(st4, "fkT", [64, T], BF16)
        von1 = sb(st4, "von1", [128, NT * 72], BF16)
        vh1 = sb(st4, "vh1", [64, T], BF16)
        amask = sb(st4, "amask", [128, 128], BF16)
        S_sb = sb(st4, "S_sb", [64, 72], BF16)
        deps = sb(st4, "deps", [1, 128], F32)
        rdep = sb(st4, "rdep", [1, 128], F32)
        rb = sb(st4, "rb", [64, 128], F32)
        ps4 = mkstack()
        pZ = ps(ps4, "pZ", [128, 256])
        pTB = ps(ps4, "pTB", [128, 1024], BF16)
        pAT = ps(ps4, "pAT", [128, 128])
        pYl = ps(ps4, "pYl", [128, 128])
        pSl = ps(ps4, "pSl", [64, 72])
        pD1 = ps(ps4, "pD1", [128, 128])
        kb.op("vector", lambda v: v.memset(von1[:, 64::72], 1.0),
              writes=[("von1", 0)])
        for hp in range(p4b_pairs):
            kb.dma("gpsimd", lambda g, hp=hp: g.dma_start(
                out=qp[:], in_=a_q1[hp * 128:(hp + 1) * 128, :]),
                reads=["a_q1"], writes=[("qp", 0)])
            kb.dma("gpsimd", lambda g, hp=hp: g.dma_start(
                out=kp[:], in_=a_k1[hp * 128:(hp + 1) * 128, :]),
                reads=["a_k1"], writes=[("kp", 0)])
            kb.dma("gpsimd", lambda g, hp=hp: g.dma_start(
                out=vp[:], in_=a_v1[hp * 128:(hp + 1) * 128, :]),
                reads=["a_v1"], writes=[("vp", 0)])
            kb.dma("sync", lambda s, hp=hp: s.dma_start(
                out=wfb[:, 0:HD], in_=c_fq[hp * 128:(hp + 1) * 128, :]),
                reads=["c_fq"], writes=[("wfb", 0)])
            kb.dma("sync", lambda s, hp=hp: s.dma_start(
                out=wfb[:, HD:2 * HD], in_=c_fk[hp * 128:(hp + 1) * 128, :]),
                reads=["c_fk"], writes=[("wfb", 1)])
            for hh in range(2):
                H = hp * 2 + hh
                kb.op("vector", lambda v, hh=hh: v.tensor_copy(
                    out=qh4[:], in_=qp[hh * 64:hh * 64 + 64, :]),
                    reads=[("qp", 0)], writes=[("qh4", 0)])
                kb.op("vector", lambda v, hh=hh: v.tensor_copy(
                    out=kh4[:], in_=kp[hh * 64:hh * 64 + 64, :]),
                    reads=[("kp", 0)], writes=[("kh4", 0)])
                kb.op("vector", lambda v, hh=hh: v.tensor_copy(
                    out=vh1[:], in_=vp[hh * 64:hh * 64 + 64, :]),
                    reads=[("vp", 0)], writes=[("vh1", 0)])
                kb.op("vector", lambda v, hh=hh: v.tensor_copy(
                    out=wfh[:, 0:HD], in_=wfb[hh * 64:hh * 64 + 64, 0:HD]),
                    reads=[("wfb", 0)], writes=[("wfh", 0)])
                kb.op("vector", lambda v, hh=hh: v.tensor_copy(
                    out=wfh[:, HD:2 * HD],
                    in_=wfb[hh * 64:hh * 64 + 64, HD:2 * HD]),
                    reads=[("wfb", 1)], writes=[("wfh", 0)])
                if "F" not in p4b_parts:
                    continue
                for lt in range(NT):
                    for s_ in range(2):
                        srcb = qh4 if s_ == 0 else kh4
                        skey = ("qh4", 0) if s_ == 0 else ("kh4", 0)
                        kb.op("tensor", lambda t, lt=lt, s_=s_, srcb=srcb:
                              t.matmul(
                                  pZ[:, s_ * 64:(s_ + 1) * 64],
                                  srcb[:, lt * 128:(lt + 1) * 128],
                                  wfh[:, s_ * 64:(s_ + 1) * 64],
                                  start=True, stop=True),
                              reads=[skey, ("wfh", 0)],
                              writes=[("pZ", 0)], inc=(s_ == 1))
                    kb.op("vector", lambda v: v.reduce_max(
                        zmax[:], pZ[:, 0:128], axis=AX.X),
                        reads=[("pZ", 0)], writes=[("zmax", 0)])
                    kb.op("vector", lambda v: v.tensor_scalar(
                        out=nzmax[:], in0=zmax[:], scalar1=-1.0,
                        scalar2=None, op0=ALU.mult),
                        reads=[("zmax", 0)], writes=[("nzmax", 0)])
                    kb.op("scalar", lambda sc: sc.activation(
                        out=ez[:, 0:128], in_=pZ[:, 0:128], func=AF.Exp,
                        bias=nzmax[:], scale=1.0),
                        reads=[("pZ", 0), ("nzmax", 0)], writes=[("ez", 0)])
                    kb.op("vector", lambda v: v.reduce_sum(
                        s4[:], ez[:, 0:128].rearrange(
                            "p (s f) -> p s f", s=2), axis=AX.X),
                        reads=[("ez", 0)], writes=[("s4", 0)])
                    kb.op("vector", lambda v: v.reciprocal(
                        r4[:], s4[:]), reads=[("s4", 0)],
                        writes=[("r4", 0)])
                    for s_ in range(2):
                        kb.op("vector", lambda v, lt=lt, s_=s_:
                              v.tensor_scalar(
                                  out=fzh[:, lt * 128 + s_ * 64:
                                          lt * 128 + (s_ + 1) * 64],
                                  in0=ez[:, s_ * 64:(s_ + 1) * 64],
                                  scalar1=r4[:, s_:s_ + 1], scalar2=None,
                                  op0=ALU.mult),
                              reads=[("ez", 0), ("r4", 0)],
                              writes=[("fzh", 0)])
                if "T" not in p4b_parts:
                    continue
                for lt in range(NT):
                    kb.op("tensor", lambda t, lt=lt: t.matmul(
                        pTB[0:64, 0:128],
                        fzh[:, lt * 128:lt * 128 + 64],
                        identity[:], start=True, stop=True,
                        is_transpose=True),
                        reads=[("fzh", 0), ("ident", 0)],
                        writes=[("pTB", 0)])
                    kb.op("vector", lambda v, lt=lt: v.tensor_copy(
                        out=fqT[:, lt * 128:(lt + 1) * 128],
                        in_=pTB[0:64, 0:128]),
                        reads=[("pTB", 0)], writes=[("fqT", 0)])
                    kb.op("tensor", lambda t, lt=lt: t.matmul(
                        pTB[0:64, 128:256],
                        fzh[:, lt * 128 + 64:lt * 128 + 128],
                        identity[:], start=True, stop=True,
                        is_transpose=True),
                        reads=[("fzh", 0), ("ident", 0)],
                        writes=[("pTB", 0)])
                    kb.op("vector", lambda v, lt=lt: v.tensor_copy(
                        out=fkT[:, lt * 128:(lt + 1) * 128],
                        in_=pTB[0:64, 128:256]),
                        reads=[("pTB", 0)], writes=[("fkT", 0)])
                    kb.op("tensor", lambda t, lt=lt: t.matmul(
                        pTB[:, 256:320],
                        vh1[:, lt * 128:(lt + 1) * 128],
                        identity[0:64, 0:64], start=True, stop=True,
                        is_transpose=True),
                        reads=[("vh1", 0), ("ident", 0)],
                        writes=[("pTB", 0)])
                    kb.op("vector", lambda v, lt=lt: v.tensor_copy(
                        out=von1[:, lt * 72:lt * 72 + 64],
                        in_=pTB[:, 256:320]),
                        reads=[("pTB", 0)], writes=[("von1", 0)])
                if "C" not in p4b_parts:
                    continue
                for c in range(NT):
                    kb.op("tensor", lambda t, c=c: t.matmul(
                        pAT[:], fkT[:, c * 128:(c + 1) * 128],
                        fqT[:, c * 128:(c + 1) * 128],
                        start=True, stop=True),
                        reads=[("fkT", 0), ("fqT", 0)], writes=[("pAT", 0)])
                    kb.op("vector", lambda v: v.tensor_tensor(
                        out=amask[:], in0=pAT[:], in1=triu[:], op=ALU.mult),
                        reads=[("pAT", 0), ("triu", 0)],
                        writes=[("amask", 0)])
                    if c > 0:
                        kb.op("tensor", lambda t, c=c: t.matmul(
                            pYl[0:65, :], S_sb[:, 0:65],
                            fqT[:, c * 128:(c + 1) * 128],
                            start=True, stop=False),
                            reads=[("S_sb", 0), ("fqT", 0)],
                            writes=[("pYl", 0)], inc=False)
                    kb.op("tensor", lambda t, c=c: t.matmul(
                        pYl[0:65, :], von1[:, c * 72:c * 72 + 65],
                        amask[:], start=(c == 0), stop=True),
                        reads=[("von1", 0), ("amask", 0)],
                        writes=[("pYl", 0)])
                    kb.op("tensor", lambda t, c=c: t.matmul(
                        pSl[:, 0:65],
                        fzh[:, c * 128 + 64:c * 128 + 128],
                        von1[:, c * 72:c * 72 + 65],
                        start=(c == 0), stop=(c == NT - 1)),
                        reads=[("fzh", 0), ("von1", 0)],
                        writes=[("pSl", 0)])
                    kb.op("vector", lambda v: v.tensor_copy(
                        out=S_sb[:, 0:65], in_=pSl[:, 0:65]),
                        reads=[("pSl", 0)], writes=[("S_sb", 0)])
                    kb.op("scalar", lambda sc: sc.activation(
                        out=deps[:], in_=pYl[64:65, :], func=AF.Copy,
                        bias=1e-12, scale=1.0), reads=[("pYl", 0)],
                        writes=[("deps", 0)])
                    kb.op("vector", lambda v: v.reciprocal(
                        rdep[:], deps[:]), reads=[("deps", 0)],
                        writes=[("rdep", 0)])
                    kb.op("tensor", lambda t: t.matmul(
                        pD1[0:64, :], ones_rf[0:1, 0:64], rdep[:],
                        start=True, stop=True),
                        reads=[("rdep", 0), ("ones", 2)],
                        writes=[("pD1", 0)])
                    kb.op("vector", lambda v: v.tensor_copy(
                        out=rb[:], in_=pD1[0:64, :]),
                        reads=[("pD1", 0)], writes=[("rb", 0)])
                    kb.op("vector", lambda v, H=H, c=c: v.tensor_tensor(
                        out=ctx1[H // 2][64 * (H % 2):64 * (H % 2) + 64,
                                         c * 128:(c + 1) * 128],
                        in0=pYl[0:64, :], in1=rb[:], op=ALU.mult),
                        reads=[("pYl", 0), ("rb", 0)],
                        writes=[("ctx1", H // 2)])
                kb.barrier()
        ps4.close()
        st4.close()

    if "P4C" in phases:
        st4c = mkstack()
        s1st = [sb(st4c, f"s1st{b}", [128, 512], BF16) for b in range(2)]
        wbigC = [sb(st4c, f"wbigC_{s}", [128, KC * 512], BF16)
                 for s in range(2)]
        emit_mm4.roll = None
        ps4c = mkstack()
        pA5 = ps(ps4c, "p5A", [128, 2048])
        pB5 = ps(ps4c, "p5B", [128, 2048])

        def ev_s1(m, tq, psl, pk):
            b = m % 2
            kb.op("vector", lambda v, psl=psl, b=b: v.tensor_copy(
                out=s1st[b][:], in_=psl), reads=[pk], writes=[("s1st", b)])
            kb.dma("gpsimd", lambda g, m=m, tq=tq, b=b: g.dma_start(
                out=a_s1[m * 128:(m + 1) * 128, tq * 512:(tq + 1) * 512],
                in_=s1st[b][:]), reads=[("s1st", b)], writes=["a_s1"])

        emit_mm4(None, D, q_sb, "ctx1", ev_s1, pA5, pB5, "p5A", "p5B",
                 bigsrc=t_Wo, bigkey="t_Wo", bigbuf=wbigC)
        kb.barrier()
        ps4c.close()
        st4c.close()
        sq_stack.close()

    if "P5" in phases:
        st5 = mkstack()
        hdn1 = [sb(st5, f"hdn1_{i}", [128, 512], BF16) for i in range(NI)]
        hg1 = [sb(st5, f"hg1{b}", [128, 512], BF16) for b in range(2)]
        y_sb = [sb(st5, f"ysb{h}", [128, 512], BF16) for h in range(KC)]
        y2t = sb(st5, "y2t", [128, 512], BF16)
        rms = sb(st5, "rms", [1, 512], F32)
        rstd1 = sb(st5, "rstd1", [1, 512], F32)
        rstdB = sb(st5, "rstdB", [128, 512], F32)
        yt = sb(st5, "yt", [128, 512], BF16)
        ssq = sb(st5, "ssq", [1, 512], F32)
        g1st = [sb(st5, f"g1st{b}", [128, 512], BF16) for b in range(2)]
        gwb1 = [sb(st5, f"gwb1{s}", [128, KC * 256], BF16) for s in range(2)]
        uwb1 = [sb(st5, f"uwb1{s}", [128, KC * 256], BF16) for s in range(2)]
        dwb1 = [sb(st5, f"dwb1{s}", [128, 11 * 512], BF16) for s in range(2)]
        ps5 = mkstack()
        pG = ps(ps5, "pG5", [128, 2048])
        pU = ps(ps5, "pU5", [128, 2048])

        def load_gu1(ib, slot):
            kb.dma("sync", lambda s, ib=ib, slot=slot: s.dma_start(
                out=gwb1[slot][:].rearrange("p (k m) -> p k m", k=KC),
                in_=t_dgu[:, ib * 256:(ib + 1) * 256].rearrange(
                    "(k p) m -> p k m", p=128)),
                reads=["t_dgu"], writes=[("gwb1", slot)])
            kb.dma("gpsimd", lambda g, ib=ib, slot=slot: g.dma_start(
                out=uwb1[slot][:].rearrange("p (k m) -> p k m", k=KC),
                in_=t_dgu[:, I0 + ib * 256:I0 + (ib + 1) * 256].rearrange(
                    "(k p) m -> p k m", p=128)),
                reads=["t_dgu"], writes=[("uwb1", slot)])

        def load_dw1(gidx, slot):
            hb4, sg = gidx // 4, gidx % 4
            kb.dma("sync" if gidx % 2 == 0 else "gpsimd",
                   lambda s, hb4=hb4, sg=sg, slot=slot: s.dma_start(
                       out=dwb1[slot][:].rearrange("p (j m) -> p j m", j=11),
                       in_=t_ddn[sg * 11 * 128:(sg + 1) * 11 * 128,
                                 hb4 * 512:(hb4 + 1) * 512].rearrange(
                           "(j p) m -> p j m", p=128)),
                   reads=["t_ddn"], writes=[("dwb1", slot)])

        for tq in range(p5_tq):
            tsl = slice(tq * 512, (tq + 1) * 512)
            kb.op("vector", lambda v: v.memset(ssq[:], 0.0),
                  writes=[("ssq", 0)])
            if "A" in p5_parts:
                load_gu1(0, 0)
            for ib in range(22 if "A" in p5_parts else 0):
                slot = ib % 2
                po = (ib % 2) * 1024
                for k in range(KC):
                    for i2 in range(2):
                        kb.op("tensor", lambda t, k=k, i2=i2, slot=slot,
                              tsl=tsl, po=po: t.matmul(
                                  pG[:, po + i2 * 512:po + (i2 + 1) * 512],
                                  gwb1[slot][:, k * 256 + i2 * 128:
                                             k * 256 + (i2 + 1) * 128],
                                  xn1T[k][:, tsl],
                                  start=(k == 0), stop=(k == KC - 1)),
                              reads=[("gwb1", slot), ("xn1T", k)],
                              writes=[("pG5", (ib % 2) * 2 + i2)],
                              inc=(i2 == 1))
                        kb.op("tensor", lambda t, k=k, i2=i2, slot=slot,
                              tsl=tsl, po=po: t.matmul(
                                  pU[:, po + i2 * 512:po + (i2 + 1) * 512],
                                  uwb1[slot][:, k * 256 + i2 * 128:
                                             k * 256 + (i2 + 1) * 128],
                                  xn1T[k][:, tsl],
                                  start=(k == 0), stop=(k == KC - 1)),
                              reads=[("uwb1", slot), ("xn1T", k)],
                              writes=[("pU5", (ib % 2) * 2 + i2)],
                              inc=(i2 == 1))
                if ib + 1 < 22:
                    load_gu1(ib + 1, (ib + 1) % 2)
                for i2 in range(2):
                    ic = ib * 2 + i2
                    hb = ic % 2
                    kb.op("scalar", lambda sc, i2=i2, hb=hb, po=po:
                          sc.activation(
                              out=hg1[hb][:],
                              in_=pG[:, po + i2 * 512:po + (i2 + 1) * 512],
                              func=AF.Silu),
                          reads=[("pG5", (ib % 2) * 2 + i2)],
                          writes=[("hg1", hb)])
                    kb.op("vector", lambda v, ic=ic, i2=i2, hb=hb, po=po:
                          v.tensor_tensor(
                              out=hdn1[ic][:],
                              in0=pU[:, po + i2 * 512:po + (i2 + 1) * 512],
                              in1=hg1[hb][:], op=ALU.mult),
                          reads=[("pU5", (ib % 2) * 2 + i2), ("hg1", hb)],
                          writes=[("hdn1", ic)])
            if "B" not in p5_parts:
                continue
            blev_ = [p for p in ("B1", "B2", "B3", "B4") if p in p5_parts]
            blev = int(blev_[0][1]) if blev_ else 4
            load_dw1(0, 0)
            for gidx in range(16):
                hb4, sg = gidx // 4, gidx % 4
                slot = gidx % 2
                for j in range(11):
                    i = sg * 11 + j
                    for h4 in range(4):
                        kb.op("tensor", lambda t, j=j, h4=h4, i=i,
                              slot=slot: t.matmul(
                                  pG[:, h4 * 512:(h4 + 1) * 512],
                                  dwb1[slot][:, j * 512 + h4 * 128:
                                             j * 512 + (h4 + 1) * 128],
                                  hdn1[i][:], start=(i == 0),
                                  stop=(i == NI - 1)),
                              reads=[("dwb1", slot), ("hdn1", i)],
                              writes=[("pG5", h4)], inc=(h4 == 3))
                if gidx + 1 < 16:
                    load_dw1(gidx + 1, (gidx + 1) % 2)
                if sg == 3 and blev >= 2:
                    for h4 in range(4):
                        ht = hb4 * 4 + h4
                        kb.op("vector", lambda v, ht=ht, h4=h4:
                              v.tensor_copy(
                                  out=y_sb[ht][:],
                                  in_=pG[:, h4 * 512:(h4 + 1) * 512]),
                              reads=[("pG5", h4)], writes=[("y", ht)])
                        if p5b2 >= 2:
                            kb.op("scalar", lambda sc, ht=ht: sc.activation(
                                out=y2t[:], in_=y_sb[ht][:],
                                func=AF.Square),
                                reads=[("y", ht)], writes=[("y2t", 0)])
                        if p5b2 >= 3:
                            kb.op("tensor", lambda t, ht=ht: t.matmul(
                                pU[0:1, 0:512], ones_c[:], y2t[:],
                                start=True, stop=True),
                                reads=[("y2t", 0), ("ones", 0)],
                                writes=[("pU5", 0)], inc=True)
                        if p5b2 >= 4:
                            kb.op("vector", lambda v: v.tensor_tensor(
                                out=ssq[:], in0=pU[0:1, 0:512], in1=ssq[:],
                                op=ALU.add),
                                reads=[("pU5", 0), ("ssq", 0)],
                                writes=[("ssq", 0)])
            if blev < 3:
                continue
            kb.op("scalar", lambda sc: sc.activation(
                out=rms[:], in_=ssq[:], func=AF.Sqrt,
                scale=1.0 / D, bias=1e-6),
                reads=[("ssq", 0)], writes=[("rms", 0)])
            kb.op("vector", lambda v: v.reciprocal(rstd1[:], rms[:]),
                  reads=[("rms", 0)], writes=[("rstd1", 0)])
            kb.op("tensor", lambda t: t.matmul(
                pU[:, 512:1024], ones_rf[:], rstd1[:], start=True,
                stop=True),
                reads=[("rstd1", 0), ("ones", 2)], writes=[("pU5", 1)])
            kb.op("vector", lambda v: v.tensor_copy(
                out=rstdB[:], in_=pU[:, 512:1024]),
                reads=[("pU5", 1)], writes=[("rstdB", 0)])
            for ht in range(KC if blev >= 4 else 0):
                b = ht % 2
                kb.op("vector", lambda v, ht=ht: v.tensor_tensor(
                    out=yt[:], in0=y_sb[ht][:], in1=rstdB[:], op=ALU.mult),
                    reads=[("y", ht), ("rstdB", 0)], writes=[("yt", 0)])
                kb.op("scalar", lambda sc, b=b: sc.activation(
                    out=g1st[b][:], in_=yt[:], func=AF.Silu),
                    reads=[("yt", 0)], writes=[("g1st", b)])
                kb.dma("gpsimd", lambda g, ht=ht, tq=tq, b=b: g.dma_start(
                    out=a_g1[ht * 128:(ht + 1) * 128,
                             tq * 512:(tq + 1) * 512],
                    in_=g1st[b][:]), reads=[("g1st", b)], writes=["a_g1"])
        kb.barrier()
        ps5.close()
        st5.close()
        sx1.close()

    if "P6" in phases:
        st6 = mkstack()
        s1t = [sb(st6, f"s1t{b}", [128, T], BF16) for b in range(2)]
        g1t = [sb(st6, f"g1t{b}", [128, T], BF16) for b in range(2)]
        sg1 = [sb(st6, f"sg1{b}", [128, T], BF16) for b in range(2)]
        for m in range(16):
            b = m % 2
            kb.dma("gpsimd", lambda g, m=m, b=b: g.dma_start(
                out=g1t[b][:], in_=a_g1[m * 128:(m + 1) * 128, :]),
                reads=["a_g1"], writes=[("g1t", b)])
            kb.dma("sync", lambda s, m=m, b=b: s.dma_start(
                out=s1t[b][:], in_=a_s1[m * 128:(m + 1) * 128, :]),
                reads=["a_s1"], writes=[("s1t", b)])
            kb.op("vector", lambda v, b=b: v.tensor_tensor(
                out=sg1[b][:], in0=s1t[b][:], in1=g1t[b][:], op=ALU.mult),
                reads=[("s1t", b), ("g1t", b)], writes=[("sg1", b)])
            kb.dma("gpsimd", lambda g, m=m, b=b: g.dma_start(
                out=a_sg1[m * 128:(m + 1) * 128, :], in_=sg1[b][:]),
                reads=[("sg1", b)], writes=["a_sg1"])
        xb3 = [sb(st6, f"xb3{b}", [128, D], F32) for b in range(2)]
        sgT3 = [sb(st6, f"sgT3{b}", [128, D], BF16) for b in range(2)]
        ot = [sb(st6, f"ot{b}", [128, D], F32) for b in range(2)]
        for t in range(NT):
            b = t % 2
            kb.dma("gpsimd", lambda g, t=t, b=b: g.dma_start(
                out=xb3[b][:], in_=a_x1[t * 128:(t + 1) * 128, :]),
                reads=["a_x1"], writes=[("xb3", b)])
            kb.dma_t("sync", lambda s, t=t, b=b: s.dma_start_transpose(
                sgT3[b][:], a_sg1[:, t * 128:(t + 1) * 128]),
                reads=["a_sg1"], writes=[("sgT3", b)])
            fence_sync()
            kb.op("vector", lambda v, b=b: v.tensor_tensor(
                out=ot[b][:], in0=xb3[b][:], in1=sgT3[b][:], op=ALU.add),
                reads=[("xb3", b), ("sgT3", b)], writes=[("ot", b)])
            kb.dma("gpsimd", lambda g, t=t, b=b: g.dma_start(
                out=out[t * 128:(t + 1) * 128, :], in_=ot[b][:]),
                reads=[("ot", b)], writes=["out"])

    kb.finalize(nc)
    for s in reversed(_stacks):
        s.close()
    root.close()
    return nc


def make_in_maps(inputs):
    def f32c(a):
        return np.ascontiguousarray(np.asarray(a), dtype=np.float32)

    X = f32c(inputs["L0_attn_x"])
    base = {
        "wq0": f32c(inputs["L0_attn_wq"]),
        "wk0": f32c(inputs["L0_attn_wk"]),
        "wv0": f32c(inputs["L0_attn_wv"]),
        "wo0": f32c(inputs["L0_attn_wo"]),
        "gw0": f32c(inputs["L0_ffn_gate_w"]),
        "uw0": f32c(inputs["L0_ffn_up_w"]),
        "dw0": f32c(inputs["L0_ffn_down_w"]),
        "gs0": f32c(inputs["L0_ffn_gate_s"]).reshape(NI, 128),
        "us0": f32c(inputs["L0_ffn_up_s"]).reshape(NI, 128),
        "ds0": f32c(inputs["L0_ffn_down_s"]).reshape(KC, 128),
        "Wq1": f32c(inputs["L1_attn_Wq"]),
        "Wk1": f32c(inputs["L1_attn_Wk"]),
        "Wv1": f32c(inputs["L1_attn_Wv"]),
        "Wo1": f32c(inputs["L1_attn_Wo"]),
        "wfq1": f32c(inputs["L1_attn_Wfq"]).reshape(NH * HD, HD),
        "wfk1": f32c(inputs["L1_attn_Wfk"]).reshape(NH * HD, HD),
        "gup": np.ascontiguousarray(
            np.asarray(inputs["L1_ffn_gate_up_packed"]), dtype=np.int32),
        "gus": np.ascontiguousarray(
            np.asarray(inputs["L1_ffn_gate_up_scales"]), dtype=np.float16),
        "dnp": np.ascontiguousarray(
            np.asarray(inputs["L1_ffn_down_packed"]), dtype=np.int32),
        "dns": np.ascontiguousarray(
            np.asarray(inputs["L1_ffn_down_scales"]), dtype=np.float16),
    }
    return [dict(base, x=np.ascontiguousarray(X[i])) for i in range(8)]


def kernel(**inputs):
    in_maps = make_in_maps(inputs)
    nc = build_program()
    res = run_bass_kernel_spmd(nc, in_maps, core_ids=list(range(8)))
    return np.stack([np.asarray(r["out"], dtype=np.float32)
                     for r in res.results], axis=0)


# version 11 (final)
# speedup vs baseline: 1.3147x; 1.3147x over previous
import numpy as np
from contextlib import ExitStack

import concourse.bass as bass
import concourse.mybir as mybir
from concourse.bass_utils import run_bass_kernel_spmd

F32 = mybir.dt.float32
BF16 = mybir.dt.bfloat16
F16 = mybir.dt.float16
I32 = mybir.dt.int32
AF = mybir.ActivationFunctionType
ALU = mybir.AluOpType
AX = mybir.AxisListType

T = 2048
D = 2048
NT = 16
KC = 16
I0 = 5632
NI = 44
O1 = 2 * I0
NO1 = 88
HD = 64
NH = 32

ENGINES = ("tensor", "vector", "scalar", "gpsimd", "sync")
DMAQ = {"sync": "dsync", "gpsimd": "dgp", "scalar": "dscal"}
ALL_SEMS = list(ENGINES) + list(DMAQ.values())

FP4C = 0xC8643210 - (1 << 32)


class KB:
    def __init__(self):
        self.stream = {e: [] for e in ENGINES}
        self.cnt = dict.fromkeys(ALL_SEMS, 0)
        self.writer = {}
        self.readers = {}
        self.guar = {e: {} for e in ENGINES}
        self.pending_t = {}
        self.n_ops = 0

    def _waits(self, engine, reads, writes):
        need = {}
        for k in reads:
            w = self.writer.get(k)
            if w is not None and need.get(w[0], -1) < w[1]:
                need[w[0]] = w[1]
        for k in writes:
            w = self.writer.get(k)
            if w is not None and need.get(w[0], -1) < w[1]:
                need[w[0]] = w[1]
            for s, v in self.readers.get(k, ()):
                if need.get(s, -1) < v:
                    need[s] = v
        g = self.guar[engine]
        for s in sorted(need):
            v = need[s]
            if s in DMAQ.values():
                v = self.cnt[s]
            if engine == "tensor" and s == "tensor":
                continue
            if g.get(s, 0) >= v:
                continue
            self.stream[engine].append(("w", s, v))
            g[s] = v

    def _register(self, sem, val, reads, writes):
        for k in writes:
            self.writer[k] = (sem, val)
            self.readers[k] = []
        for k in reads:
            self.readers.setdefault(k, []).append((sem, val))

    def op(self, engine, fn, reads=(), writes=(), inc=True):
        self._waits(engine, reads, writes)
        val = self.cnt[engine] + 1
        self.stream[engine].append(("o", fn, engine if inc else None))
        if inc:
            self.cnt[engine] = val
        self._register(engine, val, reads, writes)
        self.n_ops += 1

    def dma(self, engine, fn, reads=(), writes=()):
        self._waits(engine, reads, writes)
        q = DMAQ[engine]
        val = self.cnt[q] + 16
        self.cnt[q] = val
        self.stream[engine].append(("d", fn, q))
        self._register(q, val, reads, writes)
        self.n_ops += 1

    def dma_t(self, engine, fn, reads=(), writes=()):
        self.dma(engine, fn, reads=reads, writes=writes)

    def fence(self, engine, fn):
        pass

    def barrier(self):
        tgt = dict(self.cnt)
        for e in ENGINES:
            g = self.guar[e]
            for s, v in tgt.items():
                if v <= 0 or s == e or (e == "tensor" and s == "tensor"):
                    continue
                if g.get(s, 0) >= v:
                    continue
                self.stream[e].append(("w", s, v))
                g[s] = v

    def finalize(self, nc):
        with ExitStack() as es:
            sems = {s: es.enter_context(nc.semaphore("sem_" + s))
                    for s in ALL_SEMS}
            sems["tjunk"] = es.enter_context(nc.semaphore("sem_tjunk"))
            for s, v in self.cnt.items():
                if v > 0 and s != "sync":
                    self.stream["sync"].append(("w", s, v))
            block = es.enter_context(nc.Block())
            for e in ENGINES:
                def make(e):
                    def body(eng):
                        for it in self.stream[e]:
                            if it[0] == "w":
                                eng.wait_ge(sems[it[1]], it[2])
                            elif it[0] == "o":
                                inst = it[1](eng)
                                if it[2] is not None:
                                    inst.then_inc(sems[it[2]], 1)
                            elif it[0] == "t":
                                it[1](eng).then_inc(sems["tjunk"], 16)
                            else:
                                it[1](eng).then_inc(sems[it[2]], 16)
                    return body
                getattr(block, e)(make(e))


ALL_PHASES = ("P0", "P1", "P2", "P3", "DEC", "P4A", "P4B", "P4C", "P5", "P6")


def build_program(dbg=(), phases=ALL_PHASES, p1_tq=4, p1_parts='AB', p1_ib=22, p4b_pairs=16, p4b_parts='FTC', p5_tq=4, p5_parts='AB', p5b2=4):
    nc = bass.Bass("TRN2", target_bir_lowering=False)
    kb = KB()
    root = ExitStack()
    _stacks = []

    def mkstack():
        s = ExitStack()
        _stacks.append(s)
        return s

    def dram(name, shape, dt, inp=False):
        kind = "ExternalInput" if inp else (
            "ExternalOutput" if name in dbg else "Internal")
        return nc.dram_tensor(name, shape, dt, kind=kind).ap()

    x_in = dram("x", [T, D], F32, inp=True)
    wq0 = dram("wq0", [D, D], F32, inp=True)
    wk0 = dram("wk0", [D, 512], F32, inp=True)
    wv0 = dram("wv0", [D, 512], F32, inp=True)
    wo0 = dram("wo0", [D, D], F32, inp=True)
    gw0 = dram("gw0", [I0, D], F32, inp=True)
    uw0 = dram("uw0", [I0, D], F32, inp=True)
    dw0 = dram("dw0", [D, I0], F32, inp=True)
    gs0 = dram("gs0", [NI, 128], F32, inp=True)
    us0 = dram("us0", [NI, 128], F32, inp=True)
    ds0 = dram("ds0", [KC, 128], F32, inp=True)
    Wq1 = dram("Wq1", [D, D], F32, inp=True)
    Wk1 = dram("Wk1", [D, D], F32, inp=True)
    Wv1 = dram("Wv1", [D, D], F32, inp=True)
    Wo1 = dram("Wo1", [D, D], F32, inp=True)
    wfq1 = dram("wfq1", [NH * HD, HD], F32, inp=True)
    wfk1 = dram("wfk1", [NH * HD, HD], F32, inp=True)
    gup = dram("gup", [O1, 256], I32, inp=True)
    gus = dram("gus", [16, O1], F16, inp=True)
    dnp = dram("dnp", [D, 704], I32, inp=True)
    dns = dram("dns", [NI, D], F16, inp=True)
    out = nc.dram_tensor("out", [T, D], F32, kind="ExternalOutput").ap()

    c_gw = dram("c_gw", [I0, D], BF16)
    c_uw = dram("c_uw", [I0, D], BF16)
    c_dw = dram("c_dw", [D, I0], BF16)
    c_wq = dram("c_wq", [D, D], BF16)
    c_wk = dram("c_wk", [D, 512], BF16)
    c_wv = dram("c_wv", [D, 512], BF16)
    c_wo = dram("c_wo", [D, D], BF16)
    c_Wq = dram("c_Wq", [D, D], BF16)
    c_Wk = dram("c_Wk", [D, D], BF16)
    c_Wv = dram("c_Wv", [D, D], BF16)
    c_Wo = dram("c_Wo", [D, D], BF16)
    c_fq = dram("c_fq", [NH * HD, HD], BF16)
    c_fk = dram("c_fk", [NH * HD, HD], BF16)
    d_gu = dram("d_gu", [O1, D], BF16)
    d_dn = dram("d_dn", [D, I0], BF16)
    t_gw = dram("t_gw", [D, I0], BF16)
    t_uw = dram("t_uw", [D, I0], BF16)
    t_dw = dram("t_dw", [I0, D], BF16)
    t_dgu = dram("t_dgu", [D, O1], BF16)
    t_ddn = dram("t_ddn", [I0, D], BF16)
    t_Wq = dram("t_Wq", [D, D], BF16)
    t_Wk = dram("t_Wk", [D, D], BF16)
    t_Wv = dram("t_Wv", [D, D], BF16)
    t_Wo = dram("t_Wo", [D, D], BF16)
    a_xn0 = dram("a_xn0", [T, D], BF16)
    a_v0 = dram("a_v0", [512, T], BF16)
    a_v0T = dram("a_v0T", [T, 512], BF16)
    a_g0 = dram("a_g0", [D, T], BF16)
    a_sg0 = dram("a_sg0", [D, T], BF16)
    a_s0 = dram("a_s0", [D, T], BF16)
    a_ctx0 = dram("a_ctx0", [D, T], BF16)
    a_x1 = dram("a_x1", [T, D], F32)
    a_xn1 = dram("a_xn1", [T, D], BF16)
    a_q1 = dram("a_q1", [D, T], BF16)
    a_k1 = dram("a_k1", [D, T], BF16)
    a_v1 = dram("a_v1", [D, T], BF16)
    a_s1 = dram("a_s1", [D, T], BF16)
    a_g1 = dram("a_g1", [D, T], BF16)
    a_sg1 = dram("a_sg1", [D, T], BF16)

    def sb(stack, name, shape, dt, side="right"):
        return stack.enter_context(
            nc.sbuf_tensor(name, shape, dt, side=side))

    def ps(stack, name, shape, dt=F32):
        return stack.enter_context(nc.psum_tensor(name, shape, dt))

    iota_f = sb(root, "iota_f", [128, 128], F32, "left")
    identity = sb(root, "identity", [128, 128], BF16, "left")
    identF = sb(root, "identF", [128, 128], F32, "left")
    triu = sb(root, "triu", [128, 128], BF16, "left")
    ones_c = sb(root, "ones_c", [128, 1], BF16, "left")
    ones_r = sb(root, "ones_r", [1, 128], BF16, "left")
    ones_rf = sb(root, "ones_rf", [1, 128], F32, "left")
    gs_sb = sb(root, "gs_sb", [128, NI], F32, "left")
    us_sb = sb(root, "us_sb", [128, NI], F32, "left")
    ds_sb = sb(root, "ds_sb", [128, KC], F32, "left")
    junk = sb(root, "junk", [128, D], BF16, "left")
    stt = [{n: sb(root, f"st_{n}{b}", [128, 1], F32, "left") for n in
            ("s1", "s2", "m", "v2", "msq", "var", "std", "rstd", "mb")}
           for b in range(4)]

    kb.op("gpsimd", lambda g: g.iota(
        iota_f[:], [[1, 128]], base=0, channel_multiplier=-1,
        allow_small_or_imprecise_dtypes=True), writes=[("iota", 0)])
    kb.op("vector", lambda v: v.tensor_scalar(
        out=identity[:], in0=iota_f[:], scalar1=0.0, scalar2=None,
        op0=ALU.is_equal), reads=[("iota", 0)], writes=[("ident", 0)])
    kb.op("vector", lambda v: v.tensor_scalar(
        out=identF[:], in0=iota_f[:], scalar1=0.0, scalar2=None,
        op0=ALU.is_equal), reads=[("iota", 0)], writes=[("identF", 0)])
    kb.op("vector", lambda v: v.tensor_scalar(
        out=triu[:], in0=iota_f[:], scalar1=0.0, scalar2=None,
        op0=ALU.is_ge), reads=[("iota", 0)], writes=[("triu", 0)])
    kb.op("vector", lambda v: v.memset(ones_c[:], 1.0), writes=[("ones", 0)])
    kb.op("vector", lambda v: v.memset(ones_r[:], 1.0), writes=[("ones", 1)])
    kb.op("vector", lambda v: v.memset(ones_rf[:], 1.0), writes=[("ones", 2)])
    fence_dr = dram("fence_dr", [1, 64], F32)
    fence_sb = sb(root, "fence_sb", [1, 64], F32, "left")

    def fence_sync():
        pass

    tst = [sb(root, f"tst{b}", [128, 1408], BF16, "left") for b in range(2)]
    emit_tpass_n = [0]

    def tpass_units(src_d, src_key, dst_d, dst_key, R, C, stq="scalar",
                    ur=512):
        for cb in range(C // 128):
            for rb in range(R // ur):
                b = emit_tpass_n[0] % 2
                emit_tpass_n[0] += 1
                kb.dma_t("sync", lambda s, cb=cb, rb=rb, b=b, ur=ur:
                         s.dma_start_transpose(
                    tst[b][:, 0:ur], src_d[rb * ur:(rb + 1) * ur,
                                           cb * 128:(cb + 1) * 128]),
                    reads=[src_key], writes=[("tst", b)])
                kb.dma(stq, lambda g, cb=cb, rb=rb, b=b, ur=ur: g.dma_start(
                    out=dst_d[cb * 128:(cb + 1) * 128,
                              rb * ur:(rb + 1) * ur], in_=tst[b][:, 0:ur]),
                    reads=[("tst", b)], writes=[dst_key])
                yield

    def emit_tpass(*a, **k):
        for _ in tpass_units(*a, **k):
            pass
    for i, v in enumerate((1e-5, 1e-6, 1e-12, 0.125, 1.0 / D)):
        ct = sb(root, f"cst{i}", [128, 1], F32, "left")
        kb.op("vector", lambda ve, ct=ct, v=v: ve.memset(ct[:], v),
              writes=[("cst", i)])
        nc.const_aps.register(F32, v, ct[:])
    kb.barrier()

    class Roll:
        _n = 0

        def __init__(self, stack, shape, dt, n):
            Roll._n += 1
            self.t = [sb(stack, f"roll{Roll._n}_{i}", shape, dt)
                      for i in range(n)]
            self.n = n
            self.i = 0
            self.id = Roll._n

        def next(self):
            i = self.i % self.n
            self.i += 1
            return self.t[i], ("roll", self.id, i)

    def emit_ln(xap, xkey, dst_ap, dst_key, b):
        s = stt[b]
        kb.op("vector", lambda v, s=s, xap=xap: v.reduce_sum(
            s["s1"][:], xap, axis=AX.X), reads=[xkey], writes=[("ls1", b)])
        kb.op("scalar", lambda sc, s=s, xap=xap: sc.activation(
            out=junk[:], in_=xap, func=AF.Square, accum_out=s["s2"][:]),
            reads=[xkey], writes=[("ls2", b), ("junk", 0)])
        kb.op("vector", lambda v, s=s: v.tensor_scalar(
            out=s["m"][:], in0=s["s1"][:], scalar1=1.0 / D, scalar2=None,
            op0=ALU.mult), reads=[("ls1", b)], writes=[("lm", b)])
        kb.op("vector", lambda v, s=s: v.tensor_scalar(
            out=s["v2"][:], in0=s["s2"][:], scalar1=1.0 / D, scalar2=None,
            op0=ALU.mult), reads=[("ls2", b)], writes=[("lv2", b)])
        kb.op("vector", lambda v, s=s: v.tensor_tensor(
            out=s["msq"][:], in0=s["m"][:], in1=s["m"][:], op=ALU.mult),
            reads=[("lm", b)], writes=[("lmsq", b)])
        kb.op("vector", lambda v, s=s: v.tensor_tensor(
            out=s["var"][:], in0=s["v2"][:], in1=s["msq"][:],
            op=ALU.subtract), reads=[("lv2", b), ("lmsq", b)],
            writes=[("lvar", b)])
        kb.op("scalar", lambda sc, s=s: sc.activation(
            out=s["std"][:], in_=s["var"][:], func=AF.Sqrt, bias=1e-5),
            reads=[("lvar", b)], writes=[("lstd", b)])
        kb.op("vector", lambda v, s=s: v.reciprocal(
            s["rstd"][:], s["std"][:]), reads=[("lstd", b)],
            writes=[("lrstd", b)])
        kb.op("vector", lambda v, s=s: v.tensor_tensor(
            out=s["mb"][:], in0=s["m"][:], in1=s["rstd"][:], op=ALU.mult),
            reads=[("lm", b), ("lrstd", b)], writes=[("lmb0", b)])
        kb.op("vector", lambda v, s=s: v.tensor_scalar(
            out=s["mb"][:], in0=s["mb"][:], scalar1=-1.0, scalar2=None,
            op0=ALU.mult), reads=[("lmb0", b)], writes=[("lmb", b)])
        kb.op("vector", lambda v, s=s, xap=xap, dst_ap=dst_ap: v.tensor_scalar(
            out=dst_ap, in0=xap, scalar1=s["rstd"][:], scalar2=s["mb"][:],
            op0=ALU.mult, op1=ALU.add),
            reads=[xkey, ("lrstd", b), ("lmb", b)], writes=[dst_key])

    xn0T = None
    if "P0" in phases:
        st0 = mkstack()
        xbuf = [sb(st0, f"p0x{b}", [128, D], F32) for b in range(2)]
        xnst = [sb(st0, f"p0n{b}", [128, D], BF16) for b in range(2)]
        for t in range(NT):
            b = t % 2
            kb.dma("gpsimd", lambda g, t=t, b=b: g.dma_start(
                out=xbuf[b][:], in_=x_in[t * 128:(t + 1) * 128, :]),
                writes=[("p0x", b)])
            emit_ln(xbuf[b][:], ("p0x", b), xnst[b][:], ("p0n", b), b)
            kb.dma("gpsimd", lambda g, t=t, b=b: g.dma_start(
                out=a_xn0[t * 128:(t + 1) * 128, :], in_=xnst[b][:]),
                reads=[("p0n", b)], writes=["a_xn0"])
        kb.barrier()
        st0.close()
        build_program.sx0 = sx0 = mkstack()
        xn0T = [sb(sx0, f"xn0T_{k}", [128, T], BF16, "left")
                for k in range(KC)]
        for k in range(KC):
            kb.dma_t("sync", lambda s, k=k: s.dma_start_transpose(
                xn0T[k][:], a_xn0[:, k * 128:(k + 1) * 128]),
                reads=["a_xn0"], writes=[("xn0T", k)])
        fence_sync()

    for dst, srcw, key in (
        (c_gw, gw0, "c_gw"), (c_uw, uw0, "c_uw"), (c_dw, dw0, "c_dw"),
    ):
        kb.dma("gpsimd", lambda g, dst=dst, srcw=srcw: g.dma_start(
            out=dst, in_=srcw), writes=[key])
    for sbuf, srcw, n, key in ((gs_sb, gs0, NI, "gs"), (us_sb, us0, NI, "us"),
                               (ds_sb, ds0, KC, "ds")):
        for c in range(n):
            kb.dma("gpsimd", lambda g, sbuf=sbuf, srcw=srcw, c=c: g.dma_start(
                out=sbuf[:, c:c + 1], in_=srcw[c]), writes=[(key, c)])
    emit_tpass(c_gw, "c_gw", t_gw, "t_gw", I0, D, ur=1408)
    emit_tpass(c_uw, "c_uw", t_uw, "t_uw", I0, D, ur=1408)
    emit_tpass(c_dw, "c_dw", t_dw, "t_dw", D, I0, ur=1024)

    if "P1" in phases:
        st1 = mkstack()
        hdn = [sb(st1, f"hdn{i}", [128, 512], BF16) for i in range(NI)]
        hg = [sb(st1, f"hg{b}", [128, 512], BF16) for b in range(2)]
        g0st = [sb(st1, f"g0st{b}", [128, 512], BF16) for b in range(2)]
        gwb = [sb(st1, f"gwb{s}", [128, KC * 256], BF16) for s in range(2)]
        uwb = [sb(st1, f"uwb{s}", [128, KC * 256], BF16) for s in range(2)]
        dwb = [sb(st1, f"dwb{s}", [128, 11 * 512], BF16) for s in range(2)]
        ps1 = mkstack()
        pG = ps(ps1, "pG", [128, 2048])
        pU = ps(ps1, "pU", [128, 2048])

        def load_gu(ib, slot):
            kb.dma("sync", lambda s, ib=ib, slot=slot: s.dma_start(
                out=gwb[slot][:].rearrange("p (k m) -> p k m", k=KC),
                in_=t_gw[:, ib * 256:(ib + 1) * 256].rearrange(
                    "(k p) m -> p k m", p=128)),
                reads=["t_gw"], writes=[("gwb", slot)])
            kb.dma("gpsimd", lambda g, ib=ib, slot=slot: g.dma_start(
                out=uwb[slot][:].rearrange("p (k m) -> p k m", k=KC),
                in_=t_uw[:, ib * 256:(ib + 1) * 256].rearrange(
                    "(k p) m -> p k m", p=128)),
                reads=["t_uw"], writes=[("uwb", slot)])

        def load_dw(gidx, slot):
            hb4, sg = gidx // 4, gidx % 4
            kb.dma("sync" if gidx % 2 == 0 else "gpsimd",
                   lambda s, hb4=hb4, sg=sg, slot=slot: s.dma_start(
                       out=dwb[slot][:].rearrange("p (j m) -> p j m", j=11),
                       in_=t_dw[sg * 11 * 128:(sg + 1) * 11 * 128,
                                hb4 * 512:(hb4 + 1) * 512].rearrange(
                           "(j p) m -> p j m", p=128)),
                   reads=["t_dw"], writes=[("dwb", slot)])

        for tq in range(p1_tq if "A" in p1_parts else 0):
            tsl = slice(tq * 512, (tq + 1) * 512)
            load_gu(0, 0)
            for ib in range(p1_ib):
                slot = ib % 2
                po = (ib % 2) * 1024
                for k in range(KC):
                    for i2 in range(2):
                        kb.op("tensor", lambda t, k=k, i2=i2, slot=slot,
                              tsl=tsl, po=po: t.matmul(
                                  pG[:, po + i2 * 512:po + (i2 + 1) * 512],
                                  gwb[slot][:, k * 256 + i2 * 128:
                                            k * 256 + (i2 + 1) * 128],
                                  xn0T[k][:, tsl],
                                  start=(k == 0), stop=(k == KC - 1)),
                              reads=[("gwb", slot), ("xn0T", k)],
                              writes=[("pG", (ib % 2) * 2 + i2)],
                              inc=(i2 == 1))
                        kb.op("tensor", lambda t, k=k, i2=i2, slot=slot,
                              tsl=tsl, po=po: t.matmul(
                                  pU[:, po + i2 * 512:po + (i2 + 1) * 512],
                                  uwb[slot][:, k * 256 + i2 * 128:
                                            k * 256 + (i2 + 1) * 128],
                                  xn0T[k][:, tsl],
                                  start=(k == 0), stop=(k == KC - 1)),
                              reads=[("uwb", slot), ("xn0T", k)],
                              writes=[("pU", (ib % 2) * 2 + i2)],
                              inc=(i2 == 1))
                if ib + 1 < p1_ib:
                    load_gu(ib + 1, (ib + 1) % 2)
                for i2 in range(2):
                    ic = ib * 2 + i2
                    hb = ic % 2
                    kb.op("scalar", lambda sc, ic=ic, i2=i2, hb=hb, po=po:
                          sc.activation(
                              out=hg[hb][:],
                              in_=pG[:, po + i2 * 512:po + (i2 + 1) * 512],
                              func=AF.Silu, scale=gs_sb[:, ic:ic + 1]),
                          reads=[("pG", (ib % 2) * 2 + i2), ("gs", ic)],
                          writes=[("hg", hb)])
                    kb.op("vector", lambda v, ic=ic, i2=i2, hb=hb, po=po:
                          v.scalar_tensor_tensor(
                              out=hdn[ic][:],
                              in0=pU[:, po + i2 * 512:po + (i2 + 1) * 512],
                              scalar=us_sb[:, ic:ic + 1], in1=hg[hb][:],
                              op0=ALU.mult, op1=ALU.mult),
                          reads=[("pU", (ib % 2) * 2 + i2), ("us", ic),
                                 ("hg", hb)],
                          writes=[("hdn", ic)])
            if "B" in p1_parts:
                load_dw(0, 0)
                for gidx in range(16):
                    hb4, sg = gidx // 4, gidx % 4
                    slot = gidx % 2
                    for j in range(11):
                        i = sg * 11 + j
                        for h4 in range(4):
                            kb.op("tensor", lambda t, j=j, h4=h4, i=i,
                                  slot=slot: t.matmul(
                                      pG[:, h4 * 512:(h4 + 1) * 512],
                                      dwb[slot][:, j * 512 + h4 * 128:
                                                j * 512 + (h4 + 1) * 128],
                                      hdn[i][:], start=(i == 0),
                                      stop=(i == NI - 1)),
                                  reads=[("dwb", slot), ("hdn", i)],
                                  writes=[("pG", h4)], inc=(h4 == 3))
                    if gidx + 1 < 16:
                        load_dw(gidx + 1, (gidx + 1) % 2)
                    if sg == 3:
                        for h4 in range(4):
                            ht = hb4 * 4 + h4
                            b = ht % 2
                            kb.op("scalar", lambda sc, h4=h4, ht=ht, b=b:
                                  sc.activation(
                                      out=g0st[b][:],
                                      in_=pG[:, h4 * 512:(h4 + 1) * 512],
                                      func=AF.Silu,
                                      scale=ds_sb[:, ht:ht + 1]),
                                  reads=[("pG", h4), ("ds", ht)],
                                  writes=[("g0st", b)])
                            kb.dma("gpsimd", lambda g, ht=ht, tq=tq, b=b:
                                   g.dma_start(
                                       out=a_g0[ht * 128:(ht + 1) * 128,
                                                tq * 512:(tq + 1) * 512],
                                       in_=g0st[b][:]),
                                   reads=[("g0st", b)], writes=["a_g0"])
        kb.barrier()
        ps1.close()
        st1.close()

    def emit_mm4(load, n_out, rhs, rhs_key, evac, pA_, pB_, pAk, pBk,
                 tqs=4, transposing=False, bigsrc=None, bigkey=None,
                 bigbuf=None):
        roll_local = emit_mm4.roll
        first = True
        for tq in range(tqs):
            for mb in range(n_out // 512):
                psl, pkey = (pA_, pAk) if mb % 2 == 0 else (pB_, pBk)
                slot = mb % 2
                if bigsrc is not None:
                    if first:
                        kb.dma("sync", lambda s, mb=mb, slot=slot: s.dma_start(
                            out=bigbuf[slot][:].rearrange(
                                "p (k m) -> p k m", k=KC),
                            in_=bigsrc[:, mb * 512:(mb + 1) * 512].rearrange(
                                "(k p) m -> p k m", p=128)),
                            reads=[bigkey], writes=[("wbig", id(bigbuf), slot)])
                        first = False
                    for k in range(KC):
                        tkey = ("wbig", id(bigbuf), slot)
                        for m4 in range(4):
                            kb.op("tensor", lambda t, m4=m4, k=k, tq=tq,
                                  psl=psl, slot=slot: t.matmul(
                                      psl[:, m4 * 512:(m4 + 1) * 512],
                                      bigbuf[slot][:, k * 512 + m4 * 128:
                                                   k * 512 + (m4 + 1) * 128],
                                      rhs[k][:, tq * 512:(tq + 1) * 512],
                                      start=(k == 0), stop=(k == KC - 1)),
                                  reads=[tkey, (rhs_key, k)],
                                  writes=[(pkey, m4)], inc=(m4 == 3))
                    nmb = mb + 1 if mb + 1 < n_out // 512 else (
                        0 if tq + 1 < tqs else None)
                    if nmb is not None:
                        ns = nmb % 2
                        kb.dma("sync", lambda s, nmb=nmb, ns=ns: s.dma_start(
                            out=bigbuf[ns][:].rearrange(
                                "p (k m) -> p k m", k=KC),
                            in_=bigsrc[:, nmb * 512:(nmb + 1) * 512].rearrange(
                                "(k p) m -> p k m", p=128)),
                            reads=[bigkey], writes=[("wbig", id(bigbuf), ns)])
                else:
                    for k in range(KC):
                        tl, tkey = roll_local.next()
                        load(tl, tkey, k, mb)
                        for m4 in range(4):
                            kb.op("tensor", lambda t, tl=tl, m4=m4, k=k,
                                  tq=tq, psl=psl: t.matmul(
                                      psl[:, m4 * 512:(m4 + 1) * 512],
                                      tl[:, m4 * 128:(m4 + 1) * 128],
                                      rhs[k][:, tq * 512:(tq + 1) * 512],
                                      start=(k == 0), stop=(k == KC - 1)),
                                  reads=[tkey, (rhs_key, k)],
                                  writes=[(pkey, m4)], inc=(m4 == 3))
                for m4 in range(4):
                    evac(mb * 4 + m4, tq, psl[:, m4 * 512:(m4 + 1) * 512],
                         (pkey, m4))

    for dst, srcw, key in (
        (c_wq, wq0, "c_wq"), (c_wk, wk0, "c_wk"), (c_wv, wv0, "c_wv"),
        (c_wo, wo0, "c_wo"), (c_Wq, Wq1, "c_Wq"), (c_Wk, Wk1, "c_Wk"),
        (c_Wv, Wv1, "c_Wv"), (c_Wo, Wo1, "c_Wo"),
        (c_fq, wfq1, "c_fq"), (c_fk, wfk1, "c_fk"),
    ):
        kb.dma("gpsimd", lambda g, dst=dst, srcw=srcw: g.dma_start(
            out=dst, in_=srcw), writes=[key])
    emit_tpass(c_Wq, "c_Wq", t_Wq, "t_Wq", D, D, ur=1024)
    emit_tpass(c_Wk, "c_Wk", t_Wk, "t_Wk", D, D, ur=1024)
    emit_tpass(c_Wv, "c_Wv", t_Wv, "t_Wv", D, D, ur=1024)
    emit_tpass(c_Wo, "c_Wo", t_Wo, "t_Wo", D, D, ur=1024)

    if "P2" in phases:
        sq_stack = mkstack()
        skT = mkstack()
        st2 = mkstack()
        q_sb = [sb(sq_stack, f"q_sb{m}", [128, T], BF16) for m in range(16)]
        kT = [sb(skT, f"kT{m}", [128, T], BF16) for m in range(4)]
        vst = [sb(st2, f"vst{b}", [128, 512], BF16) for b in range(2)]
        wbig2 = [sb(st2, f"wbig2_{s}", [128, KC * 512], BF16)
                 for s in range(2)]
        emit_mm4.roll = None
        ps2 = mkstack()
        pA_ = ps(ps2, "p2A", [128, 2048])
        pB_ = ps(ps2, "p2B", [128, 2048])

        def mkload(src, skey):
            def load(tl, tkey, k, mb):
                kb.dma("gpsimd", lambda g, tl=tl, k=k, mb=mb: g.dma_start(
                    out=tl[:], in_=src[k * 128:(k + 1) * 128,
                                       mb * 512:(mb + 1) * 512]),
                    reads=[skey], writes=[tkey])
            return load

        def ev_q(m, tq, psl, pk):
            kb.op("vector", lambda v, m=m, tq=tq, psl=psl: v.tensor_copy(
                out=q_sb[m][:, tq * 512:(tq + 1) * 512], in_=psl),
                reads=[pk], writes=[("q", m)])

        def ev_k(m, tq, psl, pk):
            kb.op("vector", lambda v, m=m, tq=tq, psl=psl: v.tensor_copy(
                out=kT[m][:, tq * 512:(tq + 1) * 512], in_=psl),
                reads=[pk], writes=[("kT", m)])

        def ev_v(m, tq, psl, pk):
            b = m % 2
            kb.op("vector", lambda v, psl=psl, b=b: v.tensor_copy(
                out=vst[b][:], in_=psl), reads=[pk], writes=[("vst", b)])
            kb.dma("gpsimd", lambda g, m=m, tq=tq, b=b: g.dma_start(
                out=a_v0[m * 128:(m + 1) * 128, tq * 512:(tq + 1) * 512],
                in_=vst[b][:]), reads=[("vst", b)], writes=["a_v0"])

        emit_mm4(None, D, xn0T, "xn0T", ev_q, pA_, pB_,
                 "p2A", "p2B", bigsrc=c_wq, bigkey="c_wq", bigbuf=wbig2)
        emit_mm4(None, 512, xn0T, "xn0T", ev_k, pA_, pB_,
                 "p2A", "p2B", bigsrc=c_wk, bigkey="c_wk", bigbuf=wbig2)
        emit_mm4(None, 512, xn0T, "xn0T", ev_v, pA_, pB_,
                 "p2A", "p2B", bigsrc=c_wv, bigkey="c_wv", bigbuf=wbig2)
        for t in range(NT):
            kb.dma_t("sync", lambda s, t=t: s.dma_start_transpose(
                vst[0][:], a_v0[:, t * 128:(t + 1) * 128]),
                reads=["a_v0"], writes=[("vst", 0)])
            fence_sync()
            kb.dma("gpsimd", lambda g, t=t: g.dma_start(
                out=a_v0T[t * 128:(t + 1) * 128, :], in_=vst[0][:]),
                reads=[("vst", 0)], writes=["a_v0T"])
        kb.barrier()
        ps2.close()
        st2.close()
        build_program.sx0.close()

    def make_decode(std, pSC, pSCkey):
        gusl = sb(std, "gusl", [16, 1408], F16)
        dnsb = sb(std, "dnsb", [NI, D], F16)
        identH = sb(std, "identH", [128, 128], F16)
        scT = sb(std, "scT", [128, NO1 * 16], F32)
        scTd = sb(std, "scTd", [128, KC * NI], F32)
        kb.op("vector", lambda v: v.tensor_scalar(
            out=identH[:], in0=iota_f[:], scalar1=0.0, scalar2=None,
            op0=ALU.is_equal), reads=[("iota", 0)], writes=[("identH", 0)])
        kb.dma("sync", lambda s: s.dma_start(out=dnsb[:], in_=dns),
               writes=[("dnsb", 0)])
        for sl in range(8):
            kb.dma("sync", lambda s, sl=sl: s.dma_start(
                out=gusl[:], in_=gus[:, sl * 1408:(sl + 1) * 1408]),
                reads=[("gusl_done", 0)], writes=[("gusl", 0)])
            for cc in range(11):
                c = sl * 11 + cc
                kb.op("tensor", lambda t, cc=cc: t.matmul(
                    pSC[:, 0:16], gusl[:, cc * 128:(cc + 1) * 128],
                    identH[0:16, 0:16], start=True, stop=True),
                    reads=[("gusl", 0), ("identH", 0)], writes=[pSCkey])
                kb.op("vector", lambda v, c=c: v.tensor_copy(
                    out=scT[:, c * 16:(c + 1) * 16], in_=pSC[:, 0:16]),
                    reads=[pSCkey], writes=[("scT", 0), ("gusl_done", 0)])
        for c in range(KC):
            kb.op("tensor", lambda t, c=c: t.matmul(
                pSC[:, 0:NI], dnsb[:, c * 128:(c + 1) * 128],
                identH[0:NI, 0:NI], start=True, stop=True),
                reads=[("dnsb", 0), ("identH", 0)], writes=[pSCkey])
            kb.op("vector", lambda v, c=c: v.tensor_copy(
                out=scTd[:, c * NI:(c + 1) * NI], in_=pSC[:, 0:NI]),
                reads=[pSCkey], writes=[("scTd", 0)])

        maxw = 352
        pkt = [sb(std, f"pkt{b}", [128, maxw], I32) for b in range(2)]
        cfull = sb(std, "cfull", [128, maxw], I32)
        tmp = {}
        for n in ("sh", "v2a", "v2", "s8", "sg"):
            tmp[n] = sb(std, f"dc_{n}", [128, maxw],
                        F32 if n == "sg" else I32)
        kb.op("vector", lambda v: v.memset(cfull[:], FP4C),
              writes=[("cf", 0)])

        def tiles(packed, pkey, nwf, nsub, nch, dscr, dkey, scs, sck,
                  scnf, scp):
            nw = nwf // nsub
            scn = scnf // nsub
            dect = [sb(std, f"dect{dkey}{b}", [128, 8 * nw], BF16)
                    for b in range(2)]
            for ch in range(nch * nsub):
                c, h = ch // nsub, ch % nsub
                b = ch % 2
                kb.dma("sync", lambda s, c=c, h=h, b=b, nw=nw,
                       packed=packed: s.dma_start(
                           out=pkt[b][:, 0:nw],
                           in_=packed[c * 128:(c + 1) * 128,
                                      h * nw:(h + 1) * nw]),
                       reads=[pkey], writes=[("pkt", b)])
                for j in range(8):
                    sh = tmp["sh"][:, 0:nw]
                    v2a = tmp["v2a"][:, 0:nw]
                    v2 = tmp["v2"][:, 0:nw]
                    s8 = tmp["s8"][:, 0:nw]
                    sg = tmp["sg"][:, 0:nw]
                    ekv = ("dc", "v")
                    ekp = ("dc", "p")
                    if j == 0:
                        kb.op("vector", lambda x, b=b, sh=sh, nw=nw:
                              x.tensor_scalar(
                                  out=sh, in0=pkt[b][:, 0:nw], scalar1=2,
                                  scalar2=28, op0=ALU.logical_shift_left,
                                  op1=ALU.bitwise_and),
                              reads=[("pkt", b)], writes=[ekv])
                    else:
                        kb.op("vector", lambda x, b=b, j=j, sh=sh, nw=nw:
                              x.tensor_scalar(
                                  out=sh, in0=pkt[b][:, 0:nw],
                                  scalar1=4 * j - 2, scalar2=28,
                                  op0=ALU.logical_shift_right,
                                  op1=ALU.bitwise_and),
                              reads=[("pkt", b)], writes=[ekv])
                    kb.op("vector", lambda x, sh=sh, v2a=v2a, nw=nw:
                          x.tensor_tensor(
                              out=v2a, in0=cfull[:, 0:nw], in1=sh,
                              op=ALU.logical_shift_right),
                          reads=[ekv, ("cf", 0)], writes=[ekv])
                    kb.op("vector", lambda x, v2a=v2a, v2=v2:
                          x.tensor_scalar(
                              out=v2, in0=v2a, scalar1=15, scalar2=None,
                              op0=ALU.bitwise_and), reads=[ekv],
                          writes=[ekv])
                    kb.op("vector", lambda x, b=b, j=j, s8=s8, nw=nw:
                          x.tensor_scalar(
                              out=s8, in0=pkt[b][:, 0:nw], scalar1=4 * j,
                              scalar2=8, op0=ALU.logical_shift_right,
                              op1=ALU.bitwise_and),
                          reads=[("pkt", b)], writes=[ekv])
                    kb.op("vector", lambda x, s8=s8, sg=sg: x.tensor_scalar(
                        out=sg, in0=s8, scalar1=-0.125, scalar2=0.5,
                        op0=ALU.mult, op1=ALU.add), reads=[ekv],
                        writes=[ekp])
                    kb.op("vector", lambda x, v2=v2, sg=sg, b=b, j=j,
                          dect=dect: x.tensor_tensor(
                              out=dect[b][:, j::8], in0=v2, in1=sg,
                              op=ALU.mult),
                          reads=[ekv, ekp], writes=[("dect", dkey, b)])
                for grp in range(scn):
                    gi = c * scp + h * scn + grp
                    kb.op("scalar", lambda sc, b=b, gi=gi, grp=grp, scs=scs,
                          dect=dect: sc.mul(
                              out=dect[b][:, grp * 128:(grp + 1) * 128],
                              in_=dect[b][:, grp * 128:(grp + 1) * 128],
                              mul=scs[:, gi:gi + 1]),
                          reads=[sck, ("dect", dkey, b)],
                          writes=[("dect", dkey, b)])
                kb.dma("gpsimd", lambda g, c=c, h=h, b=b, nw=nw, dscr=dscr,
                       dect=dect: g.dma_start(
                           out=dscr[c * 128:(c + 1) * 128,
                                    h * 8 * nw:(h + 1) * 8 * nw],
                           in_=dect[b][:, 0:8 * nw]),
                       reads=[("dect", dkey, b)], writes=[dkey])
                yield

        def gen():
            yield from tiles(gup, "gup", 256, 1, NO1, d_gu, "d_gu", scT,
                             ("scT", 0), 16, 16)
            yield from tiles(dnp, "dnp", 704, 2, KC, d_dn, "d_dn", scTd,
                             ("scTd", 0), NI, NI)
        return gen()

    if "P3" in phases:
        st3 = mkstack()
        sq2 = sb(st3, "sq2", [64, T], BF16)
        kh = sb(st3, "kh", [64, T], BF16)
        qh = sb(st3, "qh", [64, T], BF16)
        normq = sb(st3, "normq", [1, T], F32)
        normk = sb(st3, "normk", [1, T], F32)
        kmax = sb(st3, "kmax", [1, 1], F32)
        negk8 = sb(st3, "negk8", [1, 1], F32)
        enB = sb(st3, "enB", [1, T], BF16)
        eBb = sb(st3, "eBb", [128, T], BF16)
        von = sb(st3, "von", [128, NT * 128], BF16)
        est = sb(st3, "est", [128, T], BF16)
        att = sb(st3, "att", [128, T], BF16)
        rec = sb(st3, "rec", [64, T], F32)
        ps3 = mkstack()
        pS = ps(ps3, "pS", [128, 2048])
        pC = ps(ps3, "pC", [128, 2048])
        std = mkstack()
        dec_gen = make_decode(std, pS, ("pS", 0)) if "DEC" in phases else None
        kb.op("vector", lambda v: v.memset(von[:], 1.0), writes=[("von", 0)])
        for g in range(8):
            if dec_gen is not None:
                for _ in range(13):
                    if next(dec_gen, "END") == "END":
                        dec_gen = None
                        break
            for c in range(NT):
                kb.dma("gpsimd", lambda gg, g=g, c=c: gg.dma_start(
                    out=von[:, c * 128:c * 128 + 64],
                    in_=a_v0T[c * 128:(c + 1) * 128,
                              g * 64:(g + 1) * 64]),
                    reads=["a_v0T"], writes=[("von", 0)])
            kb.op("vector", lambda v, g=g: v.tensor_copy(
                out=kh[:], in_=kT[g // 2][64 * (g % 2):64 * (g % 2) + 64, :]),
                reads=[("kT", g // 2)], writes=[("kh", 0)])
            kb.op("scalar", lambda sc: sc.activation(
                out=sq2[:], in_=kh[:],
                func=AF.Square), reads=[("kh", 0)], writes=[("sq2", 0)])
            for n in range(4):
                kb.op("tensor", lambda t, n=n: t.matmul(
                    pS[0:1, n * 512:(n + 1) * 512], ones_c[0:64, :],
                    sq2[:, n * 512:(n + 1) * 512], start=True, stop=True),
                    reads=[("sq2", 0), ("ones", 0)], writes=[("pS", n)],
                    inc=(n == 3))
            kb.op("scalar", lambda sc: sc.activation(
                out=normk[:], in_=pS[0:1, :], func=AF.Sqrt),
                reads=[("pS", 0), ("pS", 1), ("pS", 2), ("pS", 3)],
                writes=[("normk", 0)])
            kb.op("vector", lambda v: v.reduce_max(
                kmax[:], normk[:], axis=AX.X), reads=[("normk", 0)],
                writes=[("kmax", 0)])
            kb.op("vector", lambda v: v.tensor_scalar(
                out=negk8[:], in0=kmax[:], scalar1=-0.125, scalar2=None,
                op0=ALU.mult), reads=[("kmax", 0)], writes=[("negk8", 0)])
            for r in range(4):
                H = g * 4 + r
                hq, ho = H // 2, 64 * (H % 2)
                kb.op("vector", lambda v, hq=hq, ho=ho: v.tensor_copy(
                    out=qh[:], in_=q_sb[hq][ho:ho + 64, :]),
                    reads=[("q", hq)], writes=[("qh", 0)])
                kb.op("scalar", lambda sc: sc.activation(
                    out=sq2[:], in_=qh[:], func=AF.Square),
                    reads=[("qh", 0)], writes=[("sq2", 0)])
                for n in range(4):
                    kb.op("tensor", lambda t, n=n: t.matmul(
                        pS[0:1, n * 512:(n + 1) * 512], ones_c[0:64, :],
                        sq2[:, n * 512:(n + 1) * 512], start=True,
                        stop=True),
                        reads=[("sq2", 0), ("ones", 0)], writes=[("pS", n)],
                        inc=(n == 3))
                kb.op("scalar", lambda sc: sc.activation(
                    out=normq[:], in_=pS[0:1, :], func=AF.Sqrt),
                    reads=[("pS", 0), ("pS", 1), ("pS", 2), ("pS", 3)],
                    writes=[("normq", 0)])
                kb.op("scalar", lambda sc: sc.activation(
                    out=enB[:], in_=normq[:], func=AF.Exp, scale=negk8[:]),
                    reads=[("normq", 0), ("negk8", 0)], writes=[("enB", 0)])
                for n in range(4):
                    kb.op("tensor", lambda t, n=n: t.matmul(
                        pC[:, n * 512:(n + 1) * 512], ones_r[:],
                        enB[:, n * 512:(n + 1) * 512], start=True,
                        stop=True),
                        reads=[("enB", 0), ("ones", 1)], writes=[("pC", n)],
                        inc=(n == 3))
                kb.op("vector", lambda v: v.tensor_copy(
                    out=eBb[:], in_=pC[:]),
                    reads=[("pC", 0), ("pC", 1), ("pC", 2), ("pC", 3)],
                    writes=[("eBb", 0)])
                for kc in range(KC):
                    col0 = kc * 128
                    col = col0
                    while col < T:
                        w = min(512 - col % 512, T - col)
                        kb.op("tensor", lambda t, kc=kc, col=col, w=w:
                              t.matmul(
                                  pS[:, col:col + w],
                                  kh[:, kc * 128:(kc + 1) * 128],
                                  qh[:, col:col + w],
                                  start=True, stop=True),
                              reads=[("kh", 0), ("qh", 0)],
                              writes=[("pS", col // 512)],
                              inc=(col + w >= T))
                        col += w
                    rdk = [("pS", n) for n in range(col0 // 512, 4)]
                    kb.op("scalar", lambda sc, col0=col0: sc.activation(
                        out=est[:, col0:T], in_=pS[:, col0:T], func=AF.Exp,
                        scale=0.125), reads=rdk, writes=[("est", 0)])
                    kb.op("vector", lambda v, col0=col0: v.tensor_tensor(
                        out=att[:, col0:T], in0=est[:, col0:T],
                        in1=eBb[:, col0:T], op=ALU.mult),
                        reads=[("est", 0), ("eBb", 0)], writes=[("att", 0)])
                    kb.op("vector", lambda v, col0=col0: v.tensor_tensor(
                        out=att[:, col0:col0 + 128],
                        in0=att[:, col0:col0 + 128], in1=triu[:],
                        op=ALU.mult),
                        reads=[("att", 0), ("triu", 0)], writes=[("att", 0)])
                    col = col0
                    while col < T:
                        w = min(512 - col % 512, T - col)
                        nb = col // 512
                        kb.op("tensor", lambda t, kc=kc, col=col, w=w,
                              nb=nb: t.matmul(
                                  pC[:, col:col + w],
                                  von[:, kc * 128:(kc + 1) * 128],
                                  att[:, col:col + w], start=(kc == 0),
                                  stop=(kc == nb * 4 + 3)),
                              reads=[("von", 0), ("att", 0)],
                              writes=[("pC", nb)],
                              inc=(kc == nb * 4 + 3))
                        col += w
                kb.op("vector", lambda v: v.reciprocal(
                    rec[:], pC[64:128, :]),
                    reads=[("pC", 0), ("pC", 1), ("pC", 2), ("pC", 3)],
                    writes=[("rec", 0)])
                kb.op("vector", lambda v, hq=hq, ho=ho: v.tensor_tensor(
                    out=q_sb[hq][ho:ho + 64, :], in0=pC[0:64, :],
                    in1=rec[:], op=ALU.mult),
                    reads=[("pC", 0), ("pC", 1), ("pC", 2), ("pC", 3),
                           ("rec", 0)],
                    writes=[("q", hq)])
        if "a_ctx0" in dbg:
            for k in range(KC):
                kb.dma("gpsimd", lambda g, k=k: g.dma_start(
                    out=a_ctx0[k * 128:(k + 1) * 128, :], in_=q_sb[k][:]),
                    reads=[("q", k)], writes=["a_ctx0"])
        if dec_gen is not None:
            for _ in dec_gen:
                pass
        kb.barrier()
        ps3.close()
        std.close()
        st3.close()
        skT.close()
        st3b = mkstack()
        g0t = [sb(st3b, f"g0t{b}", [128, T], BF16) for b in range(2)]
        sgst = [sb(st3b, f"sgst{b}", [128, T], BF16) for b in range(2)]
        wbig3 = [sb(st3b, f"wbig3_{s}", [128, KC * 512], BF16)
                 for s in range(2)]
        ps3b = mkstack()
        pA3 = ps(ps3b, "p3A", [128, 2048])
        pB3 = ps(ps3b, "p3B", [128, 2048])
        kb.dma("sync", lambda s: s.dma_start(
            out=wbig3[0][:].rearrange("p (k m) -> p k m", k=KC),
            in_=c_wo[:, 0:512].rearrange("(k p) m -> p k m", p=128)),
            reads=["c_wo"], writes=[("wbig3", 0)])
        for mb in range(4):
            slot = mb % 2
            for m4 in range(4):
                m = mb * 4 + m4
                b = m % 2
                psm, pkm = (pA3, "p3A") if m % 2 == 0 else (pB3, "p3B")
                kb.dma("gpsimd", lambda g, m=m, b=b: g.dma_start(
                    out=g0t[b][:], in_=a_g0[m * 128:(m + 1) * 128, :]),
                    reads=["a_g0"], writes=[("g0t", b)])
                for k in range(KC):
                    for n in range(4):
                        kb.op("tensor", lambda t, m4=m4, n=n, k=k,
                              psm=psm, slot=slot: t.matmul(
                                  psm[:, n * 512:(n + 1) * 512],
                                  wbig3[slot][:, k * 512 + m4 * 128:
                                              k * 512 + (m4 + 1) * 128],
                                  q_sb[k][:, n * 512:(n + 1) * 512],
                                  start=(k == 0), stop=(k == KC - 1)),
                              reads=[("wbig3", slot), ("q", k)],
                              writes=[(pkm, n)],
                              inc=(n == 3))
                if m4 == 0 and mb + 1 < 4:
                    ns = (mb + 1) % 2
                    kb.dma("sync", lambda s, mb=mb, ns=ns: s.dma_start(
                        out=wbig3[ns][:].rearrange("p (k m) -> p k m", k=KC),
                        in_=c_wo[:, (mb + 1) * 512:(mb + 2) * 512].rearrange(
                            "(k p) m -> p k m", p=128)),
                        reads=["c_wo"], writes=[("wbig3", ns)])
                if "a_s0" in dbg:
                    kb.op("scalar", lambda sc, b=b, psm=psm: sc.activation(
                        out=g1dbg[b][:], in_=psm[:], func=AF.Copy),
                        reads=[(pkm, 0), (pkm, 1), (pkm, 2), (pkm, 3)],
                        writes=[("g1dbg", b)])
                    kb.dma("gpsimd", lambda g, m=m, b=b: g.dma_start(
                        out=a_s0[m * 128:(m + 1) * 128, :], in_=g1dbg[b][:]),
                        reads=[("g1dbg", b)], writes=["a_s0"])
                kb.op("vector", lambda v, b=b, psm=psm: v.tensor_tensor(
                    out=sgst[b][:], in0=psm[:], in1=g0t[b][:], op=ALU.mult),
                    reads=[(pkm, 0), (pkm, 1), (pkm, 2), (pkm, 3),
                           ("g0t", b)],
                    writes=[("sgst", b)])
                kb.dma("gpsimd", lambda g, m=m, b=b: g.dma_start(
                    out=a_sg0[m * 128:(m + 1) * 128, :], in_=sgst[b][:]),
                    reads=[("sgst", b)], writes=["a_sg0"])
        kb.barrier()
        ps3b.close()
        st3b.close()
        st3c = mkstack()
        xb2 = [sb(st3c, f"xb2{b}", [128, D], F32) for b in range(2)]
        sgT = [sb(st3c, f"sgT{b}", [128, D], BF16) for b in range(2)]
        x1t = [sb(st3c, f"x1t{b}", [128, D], F32) for b in range(2)]
        xn2 = [sb(st3c, f"xn2{b}", [128, D], BF16) for b in range(2)]
        for t in range(NT):
            b = t % 2
            kb.dma("gpsimd", lambda g, t=t, b=b: g.dma_start(
                out=xb2[b][:], in_=x_in[t * 128:(t + 1) * 128, :]),
                writes=[("xb2", b)])
            kb.dma_t("sync", lambda s, t=t, b=b: s.dma_start_transpose(
                sgT[b][:], a_sg0[:, t * 128:(t + 1) * 128]),
                reads=["a_sg0"], writes=[("sgT", b)])
            fence_sync()
            kb.op("vector", lambda v, b=b: v.tensor_tensor(
                out=x1t[b][:], in0=xb2[b][:], in1=sgT[b][:], op=ALU.add),
                reads=[("xb2", b), ("sgT", b)], writes=[("x1t", b)])
            kb.dma("gpsimd", lambda g, t=t, b=b: g.dma_start(
                out=a_x1[t * 128:(t + 1) * 128, :], in_=x1t[b][:]),
                reads=[("x1t", b)], writes=["a_x1"])
            emit_ln(x1t[b][:], ("x1t", b), xn2[b][:], ("xn2", b), 2 + b)
            kb.dma("gpsimd", lambda g, t=t, b=b: g.dma_start(
                out=a_xn1[t * 128:(t + 1) * 128, :], in_=xn2[b][:]),
                reads=[("xn2", b)], writes=["a_xn1"])
        kb.barrier()
        st3c.close()

    xn1T = None
    if "P4A" in phases:
        sx1 = mkstack()
        xn1T = [sb(sx1, f"xn1T_{k}", [128, T], BF16, "left")
                for k in range(KC)]
        for k in range(KC):
            kb.dma_t("sync", lambda s, k=k: s.dma_start_transpose(
                xn1T[k][:], a_xn1[:, k * 128:(k + 1) * 128]),
                reads=["a_xn1"], writes=[("xn1T", k)])
        fence_sync()
        st4a = mkstack()
        pst = [sb(st4a, f"pst{b}", [128, 512], BF16) for b in range(2)]
        wbigA = [sb(st4a, f"wbigA_{s}", [128, KC * 512], BF16)
                 for s in range(2)]
        emit_mm4.roll = None
        ps4a = mkstack()
        pA4 = ps(ps4a, "p4A", [128, 2048])
        pB4 = ps(ps4a, "p4B", [128, 2048])

        def mkevs(dst, dk):
            def ev(m, tq, psl, pk):
                b = m % 2
                kb.op("vector", lambda v, psl=psl, b=b: v.tensor_copy(
                    out=pst[b][:], in_=psl), reads=[pk], writes=[("pst", b)])
                kb.dma("gpsimd", lambda g, m=m, tq=tq, b=b: g.dma_start(
                    out=dst[m * 128:(m + 1) * 128,
                            tq * 512:(tq + 1) * 512],
                    in_=pst[b][:]), reads=[("pst", b)], writes=[dk])
            return ev

        emit_mm4(None, D, xn1T, "xn1T",
                 mkevs(a_q1, "a_q1"), pA4, pB4, "p4A", "p4B",
                 bigsrc=t_Wq, bigkey="t_Wq", bigbuf=wbigA)
        emit_mm4(None, D, xn1T, "xn1T",
                 mkevs(a_k1, "a_k1"), pA4, pB4, "p4A", "p4B",
                 bigsrc=t_Wk, bigkey="t_Wk", bigbuf=wbigA)
        emit_mm4(None, D, xn1T, "xn1T",
                 mkevs(a_v1, "a_v1"), pA4, pB4, "p4A", "p4B",
                 bigsrc=t_Wv, bigkey="t_Wv", bigbuf=wbigA)
        kb.barrier()
        ps4a.close()
        st4a.close()

    if "P4B" in phases:
        st4 = mkstack()
        ctx1 = q_sb
        qp = sb(st4, "qp", [128, T], BF16)
        kp = sb(st4, "kp", [128, T], BF16)
        vp = sb(st4, "vp", [128, T], BF16)
        wfb = sb(st4, "wfb", [128, 2 * HD], BF16)
        fzh = sb(st4, "fzh", [128, NT * 128], BF16)
        qh4 = sb(st4, "qh4", [64, T], BF16)
        kh4 = sb(st4, "kh4", [64, T], BF16)
        wfh = sb(st4, "wfh", [64, 2 * HD], BF16)
        ez = sb(st4, "ez", [128, 128], F32)
        zmax = sb(st4, "zmax", [128, 1], F32)
        nzmax = sb(st4, "nzmax", [128, 1], F32)
        s4 = sb(st4, "s4", [128, 2], F32)
        r4 = sb(st4, "r4", [128, 2], F32)
        fqT = sb(st4, "fqT", [64, T], BF16)
        fkT = sb(st4, "fkT", [64, T], BF16)
        von1 = sb(st4, "von1", [128, NT * 72], BF16)
        vh1 = sb(st4, "vh1", [64, T], BF16)
        amask = sb(st4, "amask", [128, 128], BF16)
        S_sb = sb(st4, "S_sb", [64, 72], BF16)
        deps = sb(st4, "deps", [1, 128], F32)
        rdep = sb(st4, "rdep", [1, 128], F32)
        rb = sb(st4, "rb", [64, 128], F32)
        ps4 = mkstack()
        pZ = ps(ps4, "pZ", [128, 256])
        pTB = ps(ps4, "pTB", [128, 1024], BF16)
        pAT = ps(ps4, "pAT", [128, 128])
        pYl = ps(ps4, "pYl", [128, 128])
        pSl = ps(ps4, "pSl", [64, 72])
        pD1 = ps(ps4, "pD1", [128, 128])
        kb.op("vector", lambda v: v.memset(von1[:, 64::72], 1.0),
              writes=[("von1", 0)])
        from itertools import chain as _chain
        tp4 = _chain(tpass_units(d_gu, "d_gu", t_dgu, "t_dgu", O1, D, ur=1408),
                     tpass_units(d_dn, "d_dn", t_ddn, "t_ddn", D, I0, ur=1024))
        for hp in range(p4b_pairs):
            kb.dma("gpsimd", lambda g, hp=hp: g.dma_start(
                out=qp[:], in_=a_q1[hp * 128:(hp + 1) * 128, :]),
                reads=["a_q1"], writes=[("qp", 0)])
            kb.dma("gpsimd", lambda g, hp=hp: g.dma_start(
                out=kp[:], in_=a_k1[hp * 128:(hp + 1) * 128, :]),
                reads=["a_k1"], writes=[("kp", 0)])
            kb.dma("gpsimd", lambda g, hp=hp: g.dma_start(
                out=vp[:], in_=a_v1[hp * 128:(hp + 1) * 128, :]),
                reads=["a_v1"], writes=[("vp", 0)])
            kb.dma("sync", lambda s, hp=hp: s.dma_start(
                out=wfb[:, 0:HD], in_=c_fq[hp * 128:(hp + 1) * 128, :]),
                reads=["c_fq"], writes=[("wfb", 0)])
            kb.dma("sync", lambda s, hp=hp: s.dma_start(
                out=wfb[:, HD:2 * HD], in_=c_fk[hp * 128:(hp + 1) * 128, :]),
                reads=["c_fk"], writes=[("wfb", 1)])
            for _ in range(14):
                if next(tp4, "E") == "E":
                    break
            for hh in range(2):
                H = hp * 2 + hh
                kb.op("vector", lambda v, hh=hh: v.tensor_copy(
                    out=qh4[:], in_=qp[hh * 64:hh * 64 + 64, :]),
                    reads=[("qp", 0)], writes=[("qh4", 0)])
                kb.op("vector", lambda v, hh=hh: v.tensor_copy(
                    out=kh4[:], in_=kp[hh * 64:hh * 64 + 64, :]),
                    reads=[("kp", 0)], writes=[("kh4", 0)])
                kb.op("vector", lambda v, hh=hh: v.tensor_copy(
                    out=vh1[:], in_=vp[hh * 64:hh * 64 + 64, :]),
                    reads=[("vp", 0)], writes=[("vh1", 0)])
                kb.op("vector", lambda v, hh=hh: v.tensor_copy(
                    out=wfh[:, 0:HD], in_=wfb[hh * 64:hh * 64 + 64, 0:HD]),
                    reads=[("wfb", 0)], writes=[("wfh", 0)])
                kb.op("vector", lambda v, hh=hh: v.tensor_copy(
                    out=wfh[:, HD:2 * HD],
                    in_=wfb[hh * 64:hh * 64 + 64, HD:2 * HD]),
                    reads=[("wfb", 1)], writes=[("wfh", 0)])
                if "F" not in p4b_parts:
                    continue
                for lt in range(NT):
                    for s_ in range(2):
                        srcb = qh4 if s_ == 0 else kh4
                        skey = ("qh4", 0) if s_ == 0 else ("kh4", 0)
                        kb.op("tensor", lambda t, lt=lt, s_=s_, srcb=srcb:
                              t.matmul(
                                  pZ[:, s_ * 64:(s_ + 1) * 64],
                                  srcb[:, lt * 128:(lt + 1) * 128],
                                  wfh[:, s_ * 64:(s_ + 1) * 64],
                                  start=True, stop=True),
                              reads=[skey, ("wfh", 0)],
                              writes=[("pZ", 0)], inc=(s_ == 1))
                    kb.op("vector", lambda v: v.reduce_max(
                        zmax[:], pZ[:, 0:128], axis=AX.X),
                        reads=[("pZ", 0)], writes=[("zmax", 0)])
                    kb.op("vector", lambda v: v.tensor_scalar(
                        out=nzmax[:], in0=zmax[:], scalar1=-1.0,
                        scalar2=None, op0=ALU.mult),
                        reads=[("zmax", 0)], writes=[("nzmax", 0)])
                    kb.op("scalar", lambda sc: sc.activation(
                        out=ez[:, 0:128], in_=pZ[:, 0:128], func=AF.Exp,
                        bias=nzmax[:], scale=1.0),
                        reads=[("pZ", 0), ("nzmax", 0)], writes=[("ez", 0)])
                    kb.op("vector", lambda v: v.reduce_sum(
                        s4[:], ez[:, 0:128].rearrange(
                            "p (s f) -> p s f", s=2), axis=AX.X),
                        reads=[("ez", 0)], writes=[("s4", 0)])
                    kb.op("vector", lambda v: v.reciprocal(
                        r4[:], s4[:]), reads=[("s4", 0)],
                        writes=[("r4", 0)])
                    for s_ in range(2):
                        kb.op("vector", lambda v, lt=lt, s_=s_:
                              v.tensor_scalar(
                                  out=fzh[:, lt * 128 + s_ * 64:
                                          lt * 128 + (s_ + 1) * 64],
                                  in0=ez[:, s_ * 64:(s_ + 1) * 64],
                                  scalar1=r4[:, s_:s_ + 1], scalar2=None,
                                  op0=ALU.mult),
                              reads=[("ez", 0), ("r4", 0)],
                              writes=[("fzh", 0)])
                if "T" not in p4b_parts:
                    continue
                for lt in range(NT):
                    kb.op("tensor", lambda t, lt=lt: t.matmul(
                        pTB[0:64, 0:128],
                        fzh[:, lt * 128:lt * 128 + 64],
                        identity[:], start=True, stop=True,
                        is_transpose=True),
                        reads=[("fzh", 0), ("ident", 0)],
                        writes=[("pTB", 0)])
                    kb.op("vector", lambda v, lt=lt: v.tensor_copy(
                        out=fqT[:, lt * 128:(lt + 1) * 128],
                        in_=pTB[0:64, 0:128]),
                        reads=[("pTB", 0)], writes=[("fqT", 0)])
                    kb.op("tensor", lambda t, lt=lt: t.matmul(
                        pTB[0:64, 128:256],
                        fzh[:, lt * 128 + 64:lt * 128 + 128],
                        identity[:], start=True, stop=True,
                        is_transpose=True),
                        reads=[("fzh", 0), ("ident", 0)],
                        writes=[("pTB", 0)])
                    kb.op("vector", lambda v, lt=lt: v.tensor_copy(
                        out=fkT[:, lt * 128:(lt + 1) * 128],
                        in_=pTB[0:64, 128:256]),
                        reads=[("pTB", 0)], writes=[("fkT", 0)])
                    kb.op("tensor", lambda t, lt=lt: t.matmul(
                        pTB[:, 256:320],
                        vh1[:, lt * 128:(lt + 1) * 128],
                        identity[0:64, 0:64], start=True, stop=True,
                        is_transpose=True),
                        reads=[("vh1", 0), ("ident", 0)],
                        writes=[("pTB", 0)])
                    kb.op("vector", lambda v, lt=lt: v.tensor_copy(
                        out=von1[:, lt * 72:lt * 72 + 64],
                        in_=pTB[:, 256:320]),
                        reads=[("pTB", 0)], writes=[("von1", 0)])
                if "C" not in p4b_parts:
                    continue
                for c in range(NT):
                    kb.op("tensor", lambda t, c=c: t.matmul(
                        pAT[:], fkT[:, c * 128:(c + 1) * 128],
                        fqT[:, c * 128:(c + 1) * 128],
                        start=True, stop=True),
                        reads=[("fkT", 0), ("fqT", 0)], writes=[("pAT", 0)])
                    kb.op("vector", lambda v: v.tensor_tensor(
                        out=amask[:], in0=pAT[:], in1=triu[:], op=ALU.mult),
                        reads=[("pAT", 0), ("triu", 0)],
                        writes=[("amask", 0)])
                    if c > 0:
                        kb.op("tensor", lambda t, c=c: t.matmul(
                            pYl[0:65, :], S_sb[:, 0:65],
                            fqT[:, c * 128:(c + 1) * 128],
                            start=True, stop=False),
                            reads=[("S_sb", 0), ("fqT", 0)],
                            writes=[("pYl", 0)], inc=False)
                    kb.op("tensor", lambda t, c=c: t.matmul(
                        pYl[0:65, :], von1[:, c * 72:c * 72 + 65],
                        amask[:], start=(c == 0), stop=True),
                        reads=[("von1", 0), ("amask", 0)],
                        writes=[("pYl", 0)])
                    kb.op("tensor", lambda t, c=c: t.matmul(
                        pSl[:, 0:65],
                        fzh[:, c * 128 + 64:c * 128 + 128],
                        von1[:, c * 72:c * 72 + 65],
                        start=(c == 0), stop=(c == NT - 1)),
                        reads=[("fzh", 0), ("von1", 0)],
                        writes=[("pSl", 0)])
                    kb.op("vector", lambda v: v.tensor_copy(
                        out=S_sb[:, 0:65], in_=pSl[:, 0:65]),
                        reads=[("pSl", 0)], writes=[("S_sb", 0)])
                    kb.op("scalar", lambda sc: sc.activation(
                        out=deps[:], in_=pYl[64:65, :], func=AF.Copy,
                        bias=1e-12, scale=1.0), reads=[("pYl", 0)],
                        writes=[("deps", 0)])
                    kb.op("vector", lambda v: v.reciprocal(
                        rdep[:], deps[:]), reads=[("deps", 0)],
                        writes=[("rdep", 0)])
                    kb.op("tensor", lambda t: t.matmul(
                        pD1[0:64, :], ones_rf[0:1, 0:64], rdep[:],
                        start=True, stop=True),
                        reads=[("rdep", 0), ("ones", 2)],
                        writes=[("pD1", 0)])
                    kb.op("vector", lambda v: v.tensor_copy(
                        out=rb[:], in_=pD1[0:64, :]),
                        reads=[("pD1", 0)], writes=[("rb", 0)])
                    kb.op("vector", lambda v, H=H, c=c: v.tensor_tensor(
                        out=ctx1[H // 2][64 * (H % 2):64 * (H % 2) + 64,
                                         c * 128:(c + 1) * 128],
                        in0=pYl[0:64, :], in1=rb[:], op=ALU.mult),
                        reads=[("pYl", 0), ("rb", 0)],
                        writes=[("ctx1", H // 2)])
                kb.barrier()
        for _ in tp4:
            pass
        ps4.close()
        st4.close()

    if "P4C" in phases:
        st4c = mkstack()
        s1st = [sb(st4c, f"s1st{b}", [128, 512], BF16) for b in range(2)]
        wbigC = [sb(st4c, f"wbigC_{s}", [128, KC * 512], BF16)
                 for s in range(2)]
        emit_mm4.roll = None
        ps4c = mkstack()
        pA5 = ps(ps4c, "p5A", [128, 2048])
        pB5 = ps(ps4c, "p5B", [128, 2048])

        def ev_s1(m, tq, psl, pk):
            b = m % 2
            kb.op("vector", lambda v, psl=psl, b=b: v.tensor_copy(
                out=s1st[b][:], in_=psl), reads=[pk], writes=[("s1st", b)])
            kb.dma("gpsimd", lambda g, m=m, tq=tq, b=b: g.dma_start(
                out=a_s1[m * 128:(m + 1) * 128, tq * 512:(tq + 1) * 512],
                in_=s1st[b][:]), reads=[("s1st", b)], writes=["a_s1"])

        emit_mm4(None, D, q_sb, "ctx1", ev_s1, pA5, pB5, "p5A", "p5B",
                 bigsrc=t_Wo, bigkey="t_Wo", bigbuf=wbigC)
        kb.barrier()
        ps4c.close()
        st4c.close()
        sq_stack.close()

    if "P5" in phases:
        st5 = mkstack()
        hdn1 = [sb(st5, f"hdn1_{i}", [128, 512], BF16) for i in range(NI)]
        hg1 = [sb(st5, f"hg1{b}", [128, 512], BF16) for b in range(2)]
        y_sb = [sb(st5, f"ysb{h}", [128, 512], BF16) for h in range(KC)]
        y2t = sb(st5, "y2t", [128, 512], BF16)
        rms = sb(st5, "rms", [1, 512], F32)
        rstd1 = sb(st5, "rstd1", [1, 512], F32)
        rstdB = sb(st5, "rstdB", [128, 512], F32)
        yt = sb(st5, "yt", [128, 512], BF16)
        ssq = sb(st5, "ssq", [1, 512], F32)
        g1st = [sb(st5, f"g1st{b}", [128, 512], BF16) for b in range(2)]
        gwb1 = [sb(st5, f"gwb1{s}", [128, KC * 256], BF16) for s in range(2)]
        uwb1 = [sb(st5, f"uwb1{s}", [128, KC * 256], BF16) for s in range(2)]
        dwb1 = [sb(st5, f"dwb1{s}", [128, 11 * 512], BF16) for s in range(2)]
        ps5 = mkstack()
        pG = ps(ps5, "pG5", [128, 2048])
        pU = ps(ps5, "pU5", [128, 2048])

        def load_gu1(ib, slot):
            kb.dma("sync", lambda s, ib=ib, slot=slot: s.dma_start(
                out=gwb1[slot][:].rearrange("p (k m) -> p k m", k=KC),
                in_=t_dgu[:, ib * 256:(ib + 1) * 256].rearrange(
                    "(k p) m -> p k m", p=128)),
                reads=["t_dgu"], writes=[("gwb1", slot)])
            kb.dma("gpsimd", lambda g, ib=ib, slot=slot: g.dma_start(
                out=uwb1[slot][:].rearrange("p (k m) -> p k m", k=KC),
                in_=t_dgu[:, I0 + ib * 256:I0 + (ib + 1) * 256].rearrange(
                    "(k p) m -> p k m", p=128)),
                reads=["t_dgu"], writes=[("uwb1", slot)])

        def load_dw1(gidx, slot):
            hb4, sg = gidx // 4, gidx % 4
            kb.dma("sync" if gidx % 2 == 0 else "gpsimd",
                   lambda s, hb4=hb4, sg=sg, slot=slot: s.dma_start(
                       out=dwb1[slot][:].rearrange("p (j m) -> p j m", j=11),
                       in_=t_ddn[sg * 11 * 128:(sg + 1) * 11 * 128,
                                 hb4 * 512:(hb4 + 1) * 512].rearrange(
                           "(j p) m -> p j m", p=128)),
                   reads=["t_ddn"], writes=[("dwb1", slot)])

        for tq in range(p5_tq):
            tsl = slice(tq * 512, (tq + 1) * 512)
            kb.op("vector", lambda v: v.memset(ssq[:], 0.0),
                  writes=[("ssq", 0)])
            if "A" in p5_parts:
                load_gu1(0, 0)
            for ib in range(22 if "A" in p5_parts else 0):
                slot = ib % 2
                po = (ib % 2) * 1024
                for k in range(KC):
                    for i2 in range(2):
                        kb.op("tensor", lambda t, k=k, i2=i2, slot=slot,
                              tsl=tsl, po=po: t.matmul(
                                  pG[:, po + i2 * 512:po + (i2 + 1) * 512],
                                  gwb1[slot][:, k * 256 + i2 * 128:
                                             k * 256 + (i2 + 1) * 128],
                                  xn1T[k][:, tsl],
                                  start=(k == 0), stop=(k == KC - 1)),
                              reads=[("gwb1", slot), ("xn1T", k)],
                              writes=[("pG5", (ib % 2) * 2 + i2)],
                              inc=(i2 == 1))
                        kb.op("tensor", lambda t, k=k, i2=i2, slot=slot,
                              tsl=tsl, po=po: t.matmul(
                                  pU[:, po + i2 * 512:po + (i2 + 1) * 512],
                                  uwb1[slot][:, k * 256 + i2 * 128:
                                             k * 256 + (i2 + 1) * 128],
                                  xn1T[k][:, tsl],
                                  start=(k == 0), stop=(k == KC - 1)),
                              reads=[("uwb1", slot), ("xn1T", k)],
                              writes=[("pU5", (ib % 2) * 2 + i2)],
                              inc=(i2 == 1))
                if ib + 1 < 22:
                    load_gu1(ib + 1, (ib + 1) % 2)
                for i2 in range(2):
                    ic = ib * 2 + i2
                    hb = ic % 2
                    kb.op("scalar", lambda sc, i2=i2, hb=hb, po=po:
                          sc.activation(
                              out=hg1[hb][:],
                              in_=pG[:, po + i2 * 512:po + (i2 + 1) * 512],
                              func=AF.Silu),
                          reads=[("pG5", (ib % 2) * 2 + i2)],
                          writes=[("hg1", hb)])
                    kb.op("vector", lambda v, ic=ic, i2=i2, hb=hb, po=po:
                          v.tensor_tensor(
                              out=hdn1[ic][:],
                              in0=pU[:, po + i2 * 512:po + (i2 + 1) * 512],
                              in1=hg1[hb][:], op=ALU.mult),
                          reads=[("pU5", (ib % 2) * 2 + i2), ("hg1", hb)],
                          writes=[("hdn1", ic)])
            if "B" not in p5_parts:
                continue
            blev_ = [p for p in ("B1", "B2", "B3", "B4") if p in p5_parts]
            blev = int(blev_[0][1]) if blev_ else 4
            load_dw1(0, 0)
            for gidx in range(16):
                hb4, sg = gidx // 4, gidx % 4
                slot = gidx % 2
                for j in range(11):
                    i = sg * 11 + j
                    for h4 in range(4):
                        kb.op("tensor", lambda t, j=j, h4=h4, i=i,
                              slot=slot: t.matmul(
                                  pG[:, h4 * 512:(h4 + 1) * 512],
                                  dwb1[slot][:, j * 512 + h4 * 128:
                                             j * 512 + (h4 + 1) * 128],
                                  hdn1[i][:], start=(i == 0),
                                  stop=(i == NI - 1)),
                              reads=[("dwb1", slot), ("hdn1", i)],
                              writes=[("pG5", h4)], inc=(h4 == 3))
                if gidx + 1 < 16:
                    load_dw1(gidx + 1, (gidx + 1) % 2)
                if sg == 3 and blev >= 2:
                    for h4 in range(4):
                        ht = hb4 * 4 + h4
                        kb.op("vector", lambda v, ht=ht, h4=h4:
                              v.tensor_copy(
                                  out=y_sb[ht][:],
                                  in_=pG[:, h4 * 512:(h4 + 1) * 512]),
                              reads=[("pG5", h4)], writes=[("y", ht)])
                        if p5b2 >= 2:
                            kb.op("scalar", lambda sc, ht=ht: sc.activation(
                                out=y2t[:], in_=y_sb[ht][:],
                                func=AF.Square),
                                reads=[("y", ht)], writes=[("y2t", 0)])
                        if p5b2 >= 3:
                            kb.op("tensor", lambda t, ht=ht: t.matmul(
                                pU[0:1, 0:512], ones_c[:], y2t[:],
                                start=True, stop=True),
                                reads=[("y2t", 0), ("ones", 0)],
                                writes=[("pU5", 0)], inc=True)
                        if p5b2 >= 4:
                            kb.op("vector", lambda v: v.tensor_tensor(
                                out=ssq[:], in0=pU[0:1, 0:512], in1=ssq[:],
                                op=ALU.add),
                                reads=[("pU5", 0), ("ssq", 0)],
                                writes=[("ssq", 0)])
            if blev < 3:
                continue
            kb.op("scalar", lambda sc: sc.activation(
                out=rms[:], in_=ssq[:], func=AF.Sqrt,
                scale=1.0 / D, bias=1e-6),
                reads=[("ssq", 0)], writes=[("rms", 0)])
            kb.op("vector", lambda v: v.reciprocal(rstd1[:], rms[:]),
                  reads=[("rms", 0)], writes=[("rstd1", 0)])
            kb.op("tensor", lambda t: t.matmul(
                pU[:, 512:1024], ones_rf[:], rstd1[:], start=True,
                stop=True),
                reads=[("rstd1", 0), ("ones", 2)], writes=[("pU5", 1)])
            kb.op("vector", lambda v: v.tensor_copy(
                out=rstdB[:], in_=pU[:, 512:1024]),
                reads=[("pU5", 1)], writes=[("rstdB", 0)])
            for ht in range(KC if blev >= 4 else 0):
                b = ht % 2
                kb.op("vector", lambda v, ht=ht: v.tensor_tensor(
                    out=yt[:], in0=y_sb[ht][:], in1=rstdB[:], op=ALU.mult),
                    reads=[("y", ht), ("rstdB", 0)], writes=[("yt", 0)])
                kb.op("scalar", lambda sc, b=b: sc.activation(
                    out=g1st[b][:], in_=yt[:], func=AF.Silu),
                    reads=[("yt", 0)], writes=[("g1st", b)])
                kb.dma("gpsimd", lambda g, ht=ht, tq=tq, b=b: g.dma_start(
                    out=a_g1[ht * 128:(ht + 1) * 128,
                             tq * 512:(tq + 1) * 512],
                    in_=g1st[b][:]), reads=[("g1st", b)], writes=["a_g1"])
        kb.barrier()
        ps5.close()
        st5.close()
        sx1.close()

    if "P6" in phases:
        st6 = mkstack()
        s1t = [sb(st6, f"s1t{b}", [128, T], BF16) for b in range(2)]
        g1t = [sb(st6, f"g1t{b}", [128, T], BF16) for b in range(2)]
        sg1 = [sb(st6, f"sg1{b}", [128, T], BF16) for b in range(2)]
        for m in range(16):
            b = m % 2
            kb.dma("gpsimd", lambda g, m=m, b=b: g.dma_start(
                out=g1t[b][:], in_=a_g1[m * 128:(m + 1) * 128, :]),
                reads=["a_g1"], writes=[("g1t", b)])
            kb.dma("sync", lambda s, m=m, b=b: s.dma_start(
                out=s1t[b][:], in_=a_s1[m * 128:(m + 1) * 128, :]),
                reads=["a_s1"], writes=[("s1t", b)])
            kb.op("vector", lambda v, b=b: v.tensor_tensor(
                out=sg1[b][:], in0=s1t[b][:], in1=g1t[b][:], op=ALU.mult),
                reads=[("s1t", b), ("g1t", b)], writes=[("sg1", b)])
            kb.dma("gpsimd", lambda g, m=m, b=b: g.dma_start(
                out=a_sg1[m * 128:(m + 1) * 128, :], in_=sg1[b][:]),
                reads=[("sg1", b)], writes=["a_sg1"])
        xb3 = [sb(st6, f"xb3{b}", [128, D], F32) for b in range(2)]
        sgT3 = [sb(st6, f"sgT3{b}", [128, D], BF16) for b in range(2)]
        ot = [sb(st6, f"ot{b}", [128, D], F32) for b in range(2)]
        for t in range(NT):
            b = t % 2
            kb.dma("gpsimd", lambda g, t=t, b=b: g.dma_start(
                out=xb3[b][:], in_=a_x1[t * 128:(t + 1) * 128, :]),
                reads=["a_x1"], writes=[("xb3", b)])
            kb.dma_t("sync", lambda s, t=t, b=b: s.dma_start_transpose(
                sgT3[b][:], a_sg1[:, t * 128:(t + 1) * 128]),
                reads=["a_sg1"], writes=[("sgT3", b)])
            fence_sync()
            kb.op("vector", lambda v, b=b: v.tensor_tensor(
                out=ot[b][:], in0=xb3[b][:], in1=sgT3[b][:], op=ALU.add),
                reads=[("xb3", b), ("sgT3", b)], writes=[("ot", b)])
            kb.dma("gpsimd", lambda g, t=t, b=b: g.dma_start(
                out=out[t * 128:(t + 1) * 128, :], in_=ot[b][:]),
                reads=[("ot", b)], writes=["out"])

    kb.finalize(nc)
    for s in reversed(_stacks):
        s.close()
    root.close()
    return nc


def make_in_maps(inputs):
    def f32c(a):
        return np.ascontiguousarray(np.asarray(a), dtype=np.float32)

    X = f32c(inputs["L0_attn_x"])
    base = {
        "wq0": f32c(inputs["L0_attn_wq"]),
        "wk0": f32c(inputs["L0_attn_wk"]),
        "wv0": f32c(inputs["L0_attn_wv"]),
        "wo0": f32c(inputs["L0_attn_wo"]),
        "gw0": f32c(inputs["L0_ffn_gate_w"]),
        "uw0": f32c(inputs["L0_ffn_up_w"]),
        "dw0": f32c(inputs["L0_ffn_down_w"]),
        "gs0": f32c(inputs["L0_ffn_gate_s"]).reshape(NI, 128),
        "us0": f32c(inputs["L0_ffn_up_s"]).reshape(NI, 128),
        "ds0": f32c(inputs["L0_ffn_down_s"]).reshape(KC, 128),
        "Wq1": f32c(inputs["L1_attn_Wq"]),
        "Wk1": f32c(inputs["L1_attn_Wk"]),
        "Wv1": f32c(inputs["L1_attn_Wv"]),
        "Wo1": f32c(inputs["L1_attn_Wo"]),
        "wfq1": f32c(inputs["L1_attn_Wfq"]).reshape(NH * HD, HD),
        "wfk1": f32c(inputs["L1_attn_Wfk"]).reshape(NH * HD, HD),
        "gup": np.ascontiguousarray(
            np.asarray(inputs["L1_ffn_gate_up_packed"]), dtype=np.int32),
        "gus": np.ascontiguousarray(
            np.asarray(inputs["L1_ffn_gate_up_scales"]), dtype=np.float16),
        "dnp": np.ascontiguousarray(
            np.asarray(inputs["L1_ffn_down_packed"]), dtype=np.int32),
        "dns": np.ascontiguousarray(
            np.asarray(inputs["L1_ffn_down_scales"]), dtype=np.float16),
    }
    return [dict(base, x=np.ascontiguousarray(X[i])) for i in range(8)]


def kernel(**inputs):
    in_maps = make_in_maps(inputs)
    nc = build_program()
    res = run_bass_kernel_spmd(nc, in_maps, core_ids=list(range(8)))
    return np.stack([np.asarray(r["out"], dtype=np.float32)
                     for r in res.results], axis=0)
